# version 41
# baseline (speedup 1.0000x reference)
import os
import numpy as np
import concourse.bass as bass
import concourse.mybir as mybir
from concourse.bass_utils import run_bass_kernel_spmd

F32 = mybir.dt.float32
BF16 = mybir.dt.bfloat16
AF = mybir.ActivationFunctionType
ALU = mybir.AluOpType
AX = mybir.AxisListType.X

NCORES = 8
NB = 2
C = 2048
D = 1024
H = 8
DT = 8
LN_EPS = 1e-5
SCALE = 0.125
CLAMP_MAX = float(np.log(1.0 / 0.01))

DEBUG_TAPS = os.environ.get("KMK_DEBUG_TAPS", "")
PH = int(os.environ.get("KMK_PH", "99"))


class Buf:
    __slots__ = ("name", "w", "readers")

    def __init__(self, name):
        self.name = name
        self.w = None
        self.readers = []


class Task:
    __slots__ = ("idx", "chan", "fn", "deps", "inc", "count", "sem")

    def __init__(self, idx, chan, fn):
        self.idx = idx
        self.chan = chan
        self.fn = fn
        self.deps = {}
        self.inc = False
        self.count = None
        self.sem = None


ENGS = ("tensor", "vector", "scalar", "gpsimd", "sync")
DMAQ = {"dma_s": "sync", "dma_g": "gpsimd", "dma_a": "scalar"}


def issue_engine(chan):
    return DMAQ.get(chan, chan)


class Graph:
    def __init__(self):
        self.tasks = []
        self.n_dma = 0

    def add(self, chan, fn, reads=(), writes=(), after=()):
        t = Task(len(self.tasks), chan, fn)
        if chan in DMAQ:
            t.sem = f"dma{self.n_dma}"
            self.n_dma += 1
        else:
            t.sem = chan
        self.tasks.append(t)
        for b in reads:
            if b.w is not None:
                t.deps[b.w] = 2
            b.readers.append(t.idx)
        for b in writes:
            if b.w is not None:
                t.deps[b.w] = max(t.deps.get(b.w, 0), 0)
            for r in b.readers:
                if r != t.idx:
                    t.deps[r] = max(t.deps.get(r, 0), 1)
            b.w = t.idx
            b.readers = []
        for p in after:
            t.deps[p] = 2
        return t.idx

    def finalize(self):
        for t in self.tasks:
            if t.chan in DMAQ:
                t.inc = True
        for t in self.tasks:
            for p, sev in t.deps.items():
                pt = self.tasks[p]
                if pt.chan in DMAQ:
                    need = True
                else:
                    need = sev >= 1 or (pt.chan != issue_engine(t.chan))
                if need:
                    pt.inc = True
        cnt = {}
        for t in self.tasks:
            if t.inc:
                step = 16 if t.chan in DMAQ else 1
                cnt[t.sem] = cnt.get(t.sem, 0) + step
                t.count = cnt[t.sem]
        self.final_counts = cnt

    def emit(self, nc, sems):
        by_eng = {e: [] for e in ENGS}
        for t in self.tasks:
            by_eng[issue_engine(t.chan)].append(t)
        watermark = {}

        def emit_engine(eng_name, eng):
            for t in by_eng[eng_name]:
                req = {}
                for p, sev in t.deps.items():
                    pt = self.tasks[p]
                    if pt.chan in DMAQ:
                        need = True
                    else:
                        need = sev >= 1 or (pt.chan != eng_name)
                    if not need:
                        continue
                    assert pt.inc and pt.count is not None
                    req[pt.sem] = max(req.get(pt.sem, 0), pt.count)
                for chan in sorted(req):
                    v = req[chan]
                    key = (eng_name, chan)
                    if watermark.get(key, 0) >= v:
                        continue
                    eng.wait_ge(sems[chan], v)
                    watermark[key] = v
                inst = t.fn(eng)
                if t.inc:
                    assert inst is not None, f"task {t.idx} on {t.chan} has no inst"
                    step = 16 if t.chan in DMAQ else 1
                    inst.then_inc(sems[t.sem], step)

        with nc.Block() as blk:
            blk.tensor(lambda e: emit_engine("tensor", e))
            blk.vector(lambda e: emit_engine("vector", e))
            blk.scalar(lambda e: emit_engine("scalar", e))
            blk.gpsimd(lambda e: emit_engine("gpsimd", e))
            blk.sync(lambda e: emit_engine("sync", e))


def _wtile(a, free):
    return np.ascontiguousarray(
        a.reshape(DT, 128, free).transpose(1, 0, 2).reshape(128, DT * free))


def host_prep(inp):
    x = inp["L0_attn_x"]
    w = {}
    mq = inp["L0_attn_Mq"]
    mqd = np.ascontiguousarray(np.broadcast_to(
        mq.reshape(D, 8, 1, 64), (D, 8, 2, 64)).reshape(D, 1024))
    w["wqd"] = _wtile(mqd, 1024)
    w["wk"] = _wtile(inp["L0_attn_Mk"], 512)
    w["wv"] = _wtile(inp["L0_attn_Mv"], 1024)
    w["wwt"] = _wtile(np.ascontiguousarray(inp["L0_ffn_W"].T), 1024)
    w["wp"] = _wtile(np.ascontiguousarray(inp["L0_ffn_Wp"].T), 256)
    ranks = (1, 2, 4, 8)
    wdpad = np.zeros((128, D), dtype=np.float32)
    wupad = np.zeros((D, 128), dtype=np.float32)
    for e, r in enumerate(ranks):
        wdpad[32 * e:32 * e + r, :] = inp[f"L0_ffn_Wd{e}"]
        wupad[:, 32 * e:32 * e + r] = inp[f"L0_ffn_Wu{e}"]
    w["wdp"] = _wtile(np.ascontiguousarray(wdpad.T), 128)
    w["wup"] = np.ascontiguousarray(wupad.T)
    w["simw"] = np.ascontiguousarray(
        inp["L0_ffn_sim"].reshape(2, 128, 4).transpose(1, 0, 2).reshape(128, 8))
    w["bpw"] = np.ascontiguousarray(inp["L0_ffn_bp"].reshape(2, 128).T)
    w["tempw"] = np.ascontiguousarray(inp["L0_ffn_temp"].reshape(1, 1))
    w["onesm"] = np.full((128, 128), 1.0 / 1024.0, dtype=np.float32)

    w["selm"] = np.ascontiguousarray(
        (np.arange(4)[:, None] == (np.arange(128)[None, :] // 32)).astype(np.float32))
    w["identc"] = np.eye(128, dtype=np.float32)
    w["onesrow"] = np.ones((1, 128), dtype=np.float32)

    in_maps = []
    for c in range(NCORES):
        m = dict(w)
        for ib in range(NB):
            xb = x[NB * c + ib]
            m[f"x{ib}"] = np.ascontiguousarray(
                xb.reshape(4, 512, 8, 128).transpose(3, 0, 2, 1).reshape(128, 16384))
        in_maps.append(m)
    return in_maps


def build_program():
    nc = bass.Bass("TRN2", target_bir_lowering=False)

    dr = {}
    for ib in range(NB):
        dr[f"x{ib}"] = nc.dram_tensor(f"x{ib}", [128, DT * C], F32,
                                      kind="ExternalInput")
        dr[f"y{ib}"] = nc.dram_tensor(f"y{ib}", [128, DT * C], F32,
                                      kind="ExternalOutput")
    for name, shape in [
        ("wqd", [128, DT * 1024]), ("wk", [128, DT * 512]), ("wv", [128, DT * 1024]),
        ("wwt", [128, DT * 1024]), ("wp", [128, DT * 256]), ("wdp", [128, DT * 128]),
        ("wup", [128, 1024]), ("simw", [128, 8]), ("bpw", [128, 2]),
        ("tempw", [1, 1]), ("onesm", [128, 128]), ("selm", [4, 128]),
        ("identc", [128, 128]), ("onesrow", [1, 128]),
    ]:
        dr[name] = nc.dram_tensor(name, shape, F32, kind="ExternalInput")

    taps = []
    from contextlib import ExitStack
    ctx = ExitStack()
    sb = {}

    def S(name, shape, dtype):
        sb[name] = ctx.enter_context(nc.sbuf_tensor("s_" + name, shape, dtype))
        return sb[name]

    with ctx:
        xT = S("xT", [128, DT * C], F32)
        hT = S("hT", [128, DT * C], BF16)
        wqd = S("wqd", [128, DT * 1024], BF16)
        wk = S("wk", [128, DT * 512], BF16)
        wv = S("wv", [128, DT * 1024], BF16)
        wwt = S("wwt", [128, DT * 1024], BF16)
        wp = S("wp", [128, DT * 256], BF16)
        wdp = S("wdp", [128, DT * 128], BF16)
        wup = S("wup", [128, 1024], BF16)
        simb = S("simb", [128, 8], BF16)
        bpb = S("bpb", [128, 2], F32)
        tempb = S("tempb", [1, 1], F32)
        onesm = S("onesm", [128, 128], BF16)
        onesm32 = S("onesm32", [128, 128], F32)
        selm = S("selm", [4, 128], BF16)
        identb = S("identb", [128, 128], BF16)
        ident32 = S("ident32", [128, 128], F32)
        onesrow = S("onesrow", [1, 128], BF16)
        xsq = [S(f"xsq{i}", [128, 512], BF16) for i in range(2)]
        lns = [S("lns0", [128, 512], F32)]
        epsb = S("epsb", [128, 1], F32)
        tinyb = S("tinyb", [128, 1], F32)
        kzp = [[S("kzE0", [128, 1024], BF16), S("kzO0", [128, 1024], BF16)]]
        m_b = S("m_b", [128, C], BF16)
        rinvb = S("rinvb", [128, C], BF16)
        qdup = [S("qdup0", [128, 2048], BF16)]
        ET = [S(f"ET{i}", [128, 2048], BF16) for i in range(4)]
        spart = S("spart", [128, 32], F32)
        ssum = S("ssum", [128, 16], F32)
        srec = S("srec", [128, 16], F32)
        projT = S("projT", [128, 2 * C], BF16)
        vh = [projT[:, 0:2048]]
        vs = [projT[:, 2048:4096]]
        sqp = [S("sqp0", [128, 512], BF16)]
        logit = S("logit", [128, 64], F32)
        sq8 = S("sq8", [128, 8], BF16)
        lmax = lns[0][:, 192:208]
        lnsim = lns[0][:, 208:212]
        tmin = lns[0][0:1, 212:213]
        texp = lns[0][0:1, 213:214]
        screp = lns[0][:, 214:215]
        rsimb = S("rsimb", [128, 4], BF16)
        texb = S("texb", [1, 1], BF16)
        simns = S("simns", [128, 8], BF16)

        ps_a = ctx.enter_context(nc.psum_tensor("ps_a", [128, 1024], F32))
        ps_b = ctx.enter_context(nc.psum_tensor("ps_b", [128, 1024], F32))
        ps_o = ctx.enter_context(nc.psum_tensor("ps_o", [128, 2048], F32))

        g = Graph()

        B_xT = [[Buf(f"xT{d}_{t}") for t in range(4)] for d in range(DT)]
        B_hT = [Buf(f"hT{d}") for d in range(DT)]
        B_w = {n: Buf(n) for n in ["wqd", "wk", "wv", "wwt", "wp", "wdp", "wup",
                                   "simb", "bpb", "tempb", "onesm", "onesm32",
                                   "selm", "identb", "onesrow", "epsb",
                                   "tinyb", "ident32"]}
        B_xsq = [Buf(f"xsq{i}") for i in range(2)]
        B_mb = [Buf("m_b_et1")] * 4
        B_rinv = [Buf("rinv_et2")] * 4
        B_lns = [Buf("lns0")]
        B_psa2 = [Buf("ps_a0"), Buf("ps_a1")]
        B_psb2 = [Buf("ps_b0"), Buf("ps_b1")]
        B_psa = B_psa2
        B_psb = B_psb2
        B_pso = Buf("ps_o")
        B_kz4 = [[Buf(f"kz{par}_{ot}") for ot in range(4)] for par in range(2)]
        B_qdup2 = [Buf("qdup_h0"), Buf("qdup_h1")]
        B_qdup = [B_qdup2[0]]
        B_vsb = [Buf(f"vs_{b}") for b in range(8)]
        B_ET = [Buf(f"ET{i}") for i in range(4)]
        B_ET[1] = B_mb[0]
        B_ET[2] = B_rinv[0]

        B_sp = [Buf(f"spart{j}") for j in range(16)]
        B_ssum = [Buf(f"ssum{b}") for b in range(8)]
        B_srec = [Buf(f"srec{b}") for b in range(8)]
        B_projT = Buf("projT")
        B_vh = [B_projT]
        B_sqp = [Buf("sqp0")]
        B_logit = Buf("logit")
        B_ev = B_lns[0]
        B_lmax = Buf("lmax")
        B_gates = B_lns[0]
        B_prep = Buf("simprep")

        def dma_cast(name_d, dst, bufs):
            g.add("dma_g", lambda e, s=dst, dn=name_d: e.dma_start(s[:], dr[dn].ap()),
                  writes=bufs)

        dma_cast("onesm", onesm, [B_w["onesm"]])
        dma_cast("wqd", wqd, [B_w["wqd"]])
        dma_cast("wk", wk, [B_w["wk"]])
        dma_cast("wv", wv, [B_w["wv"]])
        dma_cast("wwt", wwt, [B_w["wwt"]])
        dma_cast("wp", wp, [B_w["wp"]])
        dma_cast("wdp", wdp, [B_w["wdp"]])
        dma_cast("wup", wup, [B_w["wup"]])
        dma_cast("simw", simb, [B_w["simb"]])
        dma_cast("selm", selm, [B_w["selm"]])
        dma_cast("identc", identb, [B_w["identb"]])
        dma_cast("onesrow", onesrow, [B_w["onesrow"]])
        g.add("dma_s", lambda e: e.dma_start(bpb[:], dr["bpw"].ap()), writes=[B_w["bpb"]])
        g.add("dma_s", lambda e: e.dma_start(onesm32[:], dr["onesm"].ap()),
              writes=[B_w["onesm32"]])
        g.add("dma_s", lambda e: e.dma_start(ident32[:], dr["identc"].ap()),
              writes=[B_w["ident32"]])
        g.add("dma_s", lambda e: e.dma_start(tempb[:], dr["tempw"].ap()),
              writes=[B_w["tempb"]])
        g.add("vector", lambda e: e.memset(epsb[:], LN_EPS), writes=[B_w["epsb"]])
        g.add("vector", lambda e: e.memset(kzp[0][0][64:128, :], 0.0),
              writes=B_kz4[0])
        g.add("vector", lambda e: e.memset(kzp[0][1][0:64, :], 0.0),
              writes=B_kz4[1])
        g.add("vector", lambda e: e.memset(tinyb[:], 1e-30), writes=[B_w["tinyb"]])

        def prep_emit():
            g.add("vector", lambda e: e.tensor_tensor(sq8[:], simb[:], simb[:], op=ALU.mult),
                  reads=[B_w["simb"]], writes=[B_prep])
            g.add("tensor", lambda e: e.matmul(ps_a[:, 0:4], onesm[:], sq8[:, 0:4],
                                               start=True, stop=False),
                  reads=[B_prep, B_w["onesm"]], writes=[B_psa2[0]])
            g.add("tensor", lambda e: e.matmul(ps_a[:, 0:4], onesm[:], sq8[:, 4:8],
                                               start=False, stop=True),
                  reads=[B_prep, B_w["onesm"]], writes=[B_psa2[0]])
            g.add("scalar", lambda e: e.activation(lnsim[:], ps_a[:, 0:4], AF.Ln,
                                                   bias=tinyb[:], scale=1024.0),
                  reads=[B_psa2[0], B_w["tinyb"]], writes=[B_prep])
            g.add("scalar", lambda e: e.activation(rsimb[:], lnsim[:], AF.Exp,
                                                   bias=0.0, scale=-0.5),
                  reads=[B_prep], writes=[B_prep])
            g.add("vector", lambda e: e.tensor_scalar(tmin[:], tempb[:], CLAMP_MAX, None,
                                                      op0=ALU.min),
                  reads=[B_w["tempb"]], writes=[B_prep])
            g.add("scalar", lambda e: e.activation(texp[:], tmin[:], AF.Exp),
                  reads=[B_prep], writes=[B_prep])
            g.add("vector", lambda e: e.tensor_copy(texb[:], texp[:]),
                  reads=[B_prep], writes=[B_prep])
            g.add("tensor", lambda e: e.matmul(ps_b[:, 0:1], onesrow[:], texb[:],
                                               start=True, stop=True),
                  reads=[B_prep, B_w["onesrow"]], writes=[B_psb2[0]])
            g.add("vector", lambda e: e.tensor_copy(screp[:], ps_b[:, 0:1]),
                  reads=[B_psb2[0]], writes=[B_prep])
            for pt in range(2):
                g.add("vector", lambda e, pt=pt: e.tensor_tensor(
                    simns[:, 4 * pt:4 * pt + 4], simb[:, 4 * pt:4 * pt + 4], rsimb[:],
                    op=ALU.mult),
                    reads=[B_w["simb"], B_prep], writes=[B_prep])
                g.add("vector", lambda e, pt=pt: e.tensor_scalar(
                    simns[:, 4 * pt:4 * pt + 4], simns[:, 4 * pt:4 * pt + 4], screp[:],
                    None, op0=ALU.mult),
                    reads=[B_prep], writes=[B_prep])

        def xs(d, c0, c1):
            tc = c0 // 512
            assert (c1 - 1) // 512 == tc
            base = tc * 4096 + d * 512 + (c0 % 512)
            return xT[:, base:base + (c1 - c0)]

        def hs(d, c0, c1):
            return hT[:, d * C + c0:d * C + c1]

        def ln_stats(tc, sq_act=False):
            c0 = tc * 512
            mp = (ps_a if tc % 2 == 0 else ps_b)[:, 0:512]
            sp = (ps_a if tc % 2 == 0 else ps_b)[:, 512:1024]
            bpsl = B_psa2 if tc % 2 == 0 else B_psb2
            for d in range(DT):
                i = d % 2
                if sq_act:
                    g.add("scalar", lambda e, d=d, c0=c0, i=i: e.activation(
                        xsq[i][:], xs(d, c0, c0 + 512), AF.Square),
                        reads=[B_xT[d][tc]], writes=[B_xsq[i]])
                else:
                    g.add("gpsimd", lambda e, d=d, c0=c0, i=i: e.tensor_tensor(
                        xsq[i][:], xs(d, c0, c0 + 512), xs(d, c0, c0 + 512),
                        op=ALU.mult),
                        reads=[B_xT[d][tc]], writes=[B_xsq[i]])
                g.add("tensor", lambda e, mp=mp, d=d, c0=c0: e.matmul(
                    mp, onesm32[:], xs(d, c0, c0 + 512),
                    start=(d == 0), stop=(d == DT - 1)),
                    reads=[B_xT[d][tc], B_w["onesm32"]], writes=[bpsl[0]])
                g.add("tensor", lambda e, sp=sp, i=i, d=d: e.matmul(
                    sp, onesm[:], xsq[i][:], start=(d == 0), stop=(d == DT - 1)),
                    reads=[B_xsq[i], B_w["onesm"]], writes=[bpsl[1]])
            li = 0
            g.add("vector", lambda e, mp=mp, c0=c0: e.tensor_copy(
                m_b[:, c0:c0 + 512], mp),
                reads=[bpsl[0]], writes=[B_mb[tc]])
            g.add("vector", lambda e, c0=c0, li=li: e.tensor_tensor(
                lns[li][:], m_b[:, c0:c0 + 512], m_b[:, c0:c0 + 512], op=ALU.mult),
                reads=[B_mb[tc]], writes=[B_lns[li]])
            g.add("vector", lambda e, sp=sp, li=li: e.tensor_tensor(
                lns[li][:], sp, lns[li][:], op=ALU.subtract),
                reads=[bpsl[1], B_lns[li]], writes=[B_lns[li]])
            g.add("scalar", lambda e, li=li: e.activation(
                lns[li][:], lns[li][:], AF.Ln, bias=epsb[:], scale=1.0),
                reads=[B_lns[li], B_w["epsb"]], writes=[B_lns[li]])
            g.add("scalar", lambda e, c0=c0, li=li: e.activation(
                rinvb[:, c0:c0 + 512], lns[li][:], AF.Exp, bias=0.0, scale=-0.5),
                reads=[B_lns[li]], writes=[B_rinv[tc]])
        def ln_apply():
            for d in range(DT):
                hv = bass.AP(hT, d * C, [[DT * C, 128], [512, 4], [1, 512]])
                xv = bass.AP(xT, d * 512, [[DT * C, 128], [4096, 4], [1, 512]])
                mv = bass.AP(m_b, 0, [[C, 128], [512, 4], [1, 512]])
                rv = bass.AP(rinvb, 0, [[C, 128], [512, 4], [1, 512]])
                g.add("vector", lambda e, hv=hv, xv=xv, mv=mv: e.tensor_tensor(
                    hv, xv, mv, op=ALU.subtract),
                    reads=B_xT[d] + B_mb, writes=[B_hT[d]])
                g.add("vector", lambda e, d=d: e.tensor_tensor(
                    hs(d, 0, C), hs(d, 0, C), rinvb[:, :], op=ALU.mult),
                    reads=[B_hT[d]] + B_rinv, writes=[B_hT[d]])

        def ln_phase(ib):
            for tc in range(4):
                ln_stats(tc)
            ln_apply()

        def attn_head(ib, h):
            hb = 0
            c0 = h * 256
            vunits = []

            def qd_half(half, douter, via_o=False):
                if douter:
                    for d in range(DT):
                        for b4 in range(4):
                            b = half * 4 + b4
                            g.add("tensor", lambda e, b=b, b4=b4, d=d: e.matmul(
                                ps_a[:, b4 * 256:(b4 + 1) * 256],
                                wqd[:, d * 1024 + b * 128:d * 1024 + (b + 1) * 128],
                                hs(d, c0, c0 + 256),
                                start=(d == 0 and b4 % 2 == 0),
                                stop=(d == DT - 1 and b4 % 2 == 1)),
                                reads=[B_hT[d], B_w["wqd"]], writes=[B_psa2[b4 // 2]])
                else:
                    pt = ps_o if via_o else ps_a
                    ptb = [B_pso, B_pso] if via_o else B_psa2
                    for b4 in range(4):
                        b = half * 4 + b4
                        for d in range(DT):
                            g.add("tensor", lambda e, b=b, b4=b4, d=d, pt=pt:
                                  e.matmul(
                                pt[:, b4 * 256:(b4 + 1) * 256],
                                wqd[:, d * 1024 + b * 128:d * 1024 + (b + 1) * 128],
                                hs(d, c0, c0 + 256),
                                start=(d == 0), stop=(d == DT - 1)),
                                reads=[B_hT[d], B_w["wqd"]], writes=[ptb[b4 // 2]])
                    g.add("vector", lambda e, half=half, pt=pt: e.tensor_copy(
                        qdup[0][:, half * 1024:(half + 1) * 1024], pt[:, 0:1024]),
                        reads=ptb, writes=[B_qdup2[half]])
                    return
                g.add("vector", lambda e, half=half: e.tensor_copy(
                    qdup[0][:, half * 1024:(half + 1) * 1024], ps_a[:]),
                    reads=B_psa2, writes=[B_qdup2[half]])

            def k_all(douter):
                if douter:
                    for d in range(DT):
                        for ot in range(4):
                            g.add("tensor", lambda e, ot=ot, d=d: e.matmul(
                                ps_b[:, ot * 256:(ot + 1) * 256],
                                wk[:, d * 512 + ot * 128:d * 512 + (ot + 1) * 128],
                                hs(d, c0, c0 + 256),
                                start=(d == 0 and ot % 2 == 0),
                                stop=(d == DT - 1 and ot % 2 == 1)),
                                reads=[B_hT[d], B_w["wk"]], writes=[B_psb2[ot // 2]])
                else:
                    for ot in range(4):
                        for d in range(DT):
                            g.add("tensor", lambda e, ot=ot, d=d: e.matmul(
                                ps_b[:, ot * 256:(ot + 1) * 256],
                                wk[:, d * 512 + ot * 128:d * 512 + (ot + 1) * 128],
                                hs(d, c0, c0 + 256),
                                start=(d == 0), stop=(d == DT - 1)),
                                reads=[B_hT[d], B_w["wk"]], writes=[B_psb2[ot // 2]])
                for ot in range(4):
                    g.add("vector", lambda e, ot=ot: e.tensor_copy(
                        kzp[0][0][0:64, ot * 256:(ot + 1) * 256],
                        ps_b[0:64, ot * 256:(ot + 1) * 256]),
                        reads=[B_psb2[ot // 2]], writes=[B_kz4[0][ot]])
                    g.add("vector", lambda e, ot=ot: e.tensor_copy(
                        kzp[0][1][64:128, ot * 256:(ot + 1) * 256],
                        ps_b[64:128, ot * 256:(ot + 1) * 256]),
                        reads=[B_psb2[ot // 2]], writes=[B_kz4[1][ot]])

            first = (h == 0)
            if first:
                for d in range(DT):
                    for b4 in range(4):
                        g.add("tensor", lambda e, b4=b4, d=d: e.matmul(
                            ps_a[:, b4 * 256:(b4 + 1) * 256],
                            wqd[:, d * 1024 + b4 * 128:d * 1024 + (b4 + 1) * 128],
                            hs(d, c0, c0 + 256),
                            start=(d == 0 and b4 % 2 == 0),
                            stop=(d == DT - 1 and b4 % 2 == 1)),
                            reads=[B_hT[d], B_w["wqd"]], writes=[B_psa2[b4 // 2]])
                    for ot in range(4):
                        g.add("tensor", lambda e, ot=ot, d=d: e.matmul(
                            ps_b[:, ot * 256:(ot + 1) * 256],
                            wk[:, d * 512 + ot * 128:d * 512 + (ot + 1) * 128],
                            hs(d, c0, c0 + 256),
                            start=(d == 0 and ot % 2 == 0),
                            stop=(d == DT - 1 and ot % 2 == 1)),
                            reads=[B_hT[d], B_w["wk"]], writes=[B_psb2[ot // 2]])
                    for s4 in range(4):
                        tt, s = s4 // 2, s4 % 2
                        g.add("tensor", lambda e, tt=tt, d=d, s=s: e.matmul(
                            ps_o[:, tt * 1024 + s * 512:tt * 1024 + (s + 1) * 512],
                            hs(d, c0 + tt * 128, c0 + (tt + 1) * 128),
                            wv[:, d * 1024 + s * 512:d * 1024 + (s + 1) * 512],
                            start=(d == 0), stop=(d == DT - 1)),
                            reads=[B_hT[d], B_w["wv"]], writes=[B_pso])
                g.add("vector", lambda e: e.tensor_copy(
                    qdup[0][:, 0:1024], ps_a[:]),
                    reads=B_psa2, writes=[B_qdup2[0]])
                for ot in range(4):
                    g.add("vector", lambda e, ot=ot: e.tensor_copy(
                        kzp[0][0][0:64, ot * 256:(ot + 1) * 256],
                        ps_b[0:64, ot * 256:(ot + 1) * 256]),
                        reads=[B_psb2[ot // 2]], writes=[B_kz4[0][ot]])
                    g.add("vector", lambda e, ot=ot: e.tensor_copy(
                        kzp[0][1][64:128, ot * 256:(ot + 1) * 256],
                        ps_b[64:128, ot * 256:(ot + 1) * 256]),
                        reads=[B_psb2[ot // 2]], writes=[B_kz4[1][ot]])
                g.add("vector", lambda e: e.tensor_copy(vh[hb][:], ps_o[:]),
                      reads=[B_pso], writes=[B_projT])
                qd_half(1, False)
            else:
                qd_half(0, False)
                k_all(False)
                qd_half(1, False, via_o=True)
                for tt in range(2):
                    for s in range(2):
                        unit = []
                        for d in range(DT):
                            unit.append(lambda tt=tt, d=d, s=s: g.add(
                                "tensor", lambda e, tt=tt, d=d, s=s: e.matmul(
                                    ps_o[:, tt * 1024 + s * 512:
                                         tt * 1024 + (s + 1) * 512],
                                    hs(d, c0 + tt * 128, c0 + (tt + 1) * 128),
                                    wv[:, d * 1024 + s * 512:
                                       d * 1024 + (s + 1) * 512],
                                    start=(d == 0), stop=(d == DT - 1)),
                                reads=[B_hT[d], B_w["wv"]], writes=[B_pso]))
                        if s == 1:
                            unit.append(lambda tt=tt: g.add(
                                "vector", lambda e, tt=tt: e.tensor_copy(
                                    vh[hb][:, tt * 1024:(tt + 1) * 1024],
                                    ps_o[:, tt * 1024:(tt + 1) * 1024]),
                                reads=[B_pso], writes=[B_projT]))
                        vunits.append(unit)
            pend = []

            def push_out(bj):
                for j2 in (2 * bj, 2 * bj + 1):
                    vcol = (j2 % 2) * 1024 + (j2 // 2) * 128
                    for s in range(4):
                        pend.append((j2, lambda j2=j2, vcol=vcol, s=s: g.add(
                            "tensor", lambda e, j2=j2, vcol=vcol, s=s: e.matmul(
                                ps_o[:, s * 512:(s + 1) * 512],
                                vs[0][:, vcol:vcol + 128],
                                ET[j2 % 4][:, s * 512:(s + 1) * 512],
                                start=(j2 == 0), stop=(j2 == 15)),
                            reads=[B_vsb[j2 // 2], B_ET[j2 % 4]], writes=[B_pso])))

            def drain_out(n):
                for _ in range(min(n, len(pend))):
                    pend.pop(0)[1]()

            for jt in range(16):
                while pend and pend[0][0] <= jt - 4:
                    pend.pop(0)[1]()
                pj = (jt // 2) % 2
                lhs_c = (jt // 4) * 256 + (jt % 2) * 128
                for ihf in range(2):
                    pbuf = (ps_a, ps_b)[(2 * jt + ihf) % 2]
                    bl = (B_psa2, B_psb2)[(2 * jt + ihf) % 2]
                    for half in range(2):
                        g.add("tensor", lambda e, pbuf=pbuf, pj=pj, lhs_c=lhs_c,
                              half=half, ihf=ihf: e.matmul(
                                  pbuf[:, half * 512:(half + 1) * 512],
                                  kzp[0][pj][:, lhs_c:lhs_c + 128],
                                  qdup[0][:, ihf * 1024 + half * 512:
                                       ihf * 1024 + (half + 1) * 512],
                                  start=True, stop=True),
                              reads=[B_kz4[pj][jt // 4], B_qdup2[ihf]],
                              writes=[bl[half]])
                    g.add("scalar", lambda e, pbuf=pbuf, jt=jt, ihf=ihf: e.activation(
                        ET[jt % 4][:, ihf * 1024:(ihf + 1) * 1024], pbuf[:], AF.Exp,
                        bias=0.0, scale=SCALE,
                        accum_out=spart[:, 2 * jt + ihf:2 * jt + ihf + 1]),
                        reads=bl, writes=[B_ET[jt % 4], B_sp[jt]])
                    if vunits:
                        for th in vunits.pop(0):
                            th()
                    else:
                        ndr = 0
                        while pend and len(pend) > 16 and ndr < 2:
                            pend.pop(0)[1]()
                            ndr += 1
                if jt % 2 == 1:
                    bj = jt // 2
                    g.add("vector", lambda e, bj=bj: e.tensor_tensor(
                        ssum[:, 2 * bj:2 * bj + 2],
                        spart[:, 4 * bj:4 * bj + 4:2],
                        spart[:, 4 * bj + 1:4 * bj + 4:2], op=ALU.add),
                        reads=[B_sp[jt - 1], B_sp[jt]], writes=[B_ssum[bj]])
                    g.add("vector", lambda e, bj=bj: e.reciprocal(
                        srec[:, 2 * bj:2 * bj + 2], ssum[:, 2 * bj:2 * bj + 2]),
                        reads=[B_ssum[bj]], writes=[B_srec[bj]])
                    g.add("vector", lambda e, bj=bj: e.tensor_tensor(
                        bass.AP(sb["projT"], 2048 + bj * 128,
                                [[2 * C, 128], [1024, 2], [1, 128]]),
                        bass.AP(sb["projT"], bj * 128,
                                [[2 * C, 128], [1024, 2], [1, 128]]),
                        srec[:, 2 * bj:2 * bj + 2].unsqueeze(2)
                        .broadcast_to([128, 2, 128]), op=ALU.mult),
                        reads=[B_projT, B_srec[bj]], writes=[B_vsb[bj]])
                    push_out(bj)
            drain_out(len(pend))
            xview = bass.AP(xT, (h // 2) * 4096 + (h % 2) * 256,
                            [[DT * C, 128], [512, 8], [1, 256]])
            oview = bass.AP(ps_o, 0, [[2048, 128], [256, 8], [1, 256]])
            xbufs = [B_xT[b][h // 2] for b in range(8)]
            g.add("vector", lambda e, xview=xview, oview=oview: e.tensor_tensor(
                xview, xview, oview, op=ALU.add),
                reads=[B_pso] + xbufs, writes=xbufs)

        def ffn_phase(ib):
            def p1ps(s):
                return (ps_a, ps_a, ps_b, ps_b)[s][:, (s % 2) * 512:
                                                   (s % 2) * 512 + 512]

            def p1buf(s):
                return (B_psa2[0], B_psa2[1], B_psb2[0], B_psb2[1])[s]
            for d in range(DT):
                for s in range(4):
                    g.add("tensor", lambda e, d=d, s=s: e.matmul(
                        ps_o[:, s * 512:(s + 1) * 512],
                        wp[:, d * 256:d * 256 + 128],
                        hs(d, s * 512, (s + 1) * 512),
                        start=(d == 0), stop=(d == DT - 1)),
                        reads=[B_hT[d], B_w["wp"]], writes=[B_pso])
                for s in range(4):
                    g.add("tensor", lambda e, d=d, s=s: e.matmul(
                        p1ps(s),
                        wp[:, d * 256 + 128:d * 256 + 256],
                        hs(d, s * 512, (s + 1) * 512),
                        start=(d == 0), stop=(d == DT - 1)),
                        reads=[B_hT[d], B_w["wp"]], writes=[p1buf(s)])
            g.add("vector", lambda e: e.tensor_scalar(
                projT[:, 0:C], ps_o[:], bpb[:, 0:1], None, op0=ALU.add),
                reads=[B_pso, B_w["bpb"]], writes=[B_projT] + B_vsb)
            g.add("vector", lambda e: e.tensor_scalar(
                projT[:, C:C + 1024], ps_a[:], bpb[:, 1:2], None, op0=ALU.add),
                reads=B_psa2 + [B_w["bpb"]], writes=[B_projT])
            g.add("vector", lambda e: e.tensor_scalar(
                projT[:, C + 1024:2 * C], ps_b[:], bpb[:, 1:2], None, op0=ALU.add),
                reads=B_psb2 + [B_w["bpb"]], writes=[B_projT])
            for d in range(DT // 2):
                for s in range(4):
                    g.add("tensor", lambda e, d=d, s=s: e.matmul(
                        ps_o[:, s * 512:(s + 1) * 512], wdp[:, d * 128:(d + 1) * 128],
                        hs(d, s * 512, (s + 1) * 512),
                        start=(d == 0), stop=False),
                        reads=[B_hT[d], B_w["wdp"]], writes=[B_pso])
            for tc in range(4):
                c0 = tc * 512
                nps = (ps_a if tc % 2 == 0 else ps_b)[:, 0:512]
                bps = B_psa2[0] if tc % 2 == 0 else B_psb2[0]
                for pt in range(2):
                    i = 0
                    g.add("gpsimd", lambda e, pt=pt, c0=c0, i=i: e.tensor_tensor(
                        sqp[i][:], projT[:, pt * C + c0:pt * C + c0 + 512],
                        projT[:, pt * C + c0:pt * C + c0 + 512], op=ALU.mult),
                        reads=[B_projT], writes=[B_sqp[i]])
                    g.add("tensor", lambda e, nps=nps, i=i, pt=pt: e.matmul(
                        nps, onesm[:], sqp[i][:], start=(pt == 0), stop=(pt == 1)),
                        reads=[B_sqp[i], B_w["onesm"]], writes=[bps])
                li = 0
                g.add("scalar", lambda e, nps=nps, li=li: e.activation(
                    lns[li][:], nps, AF.Ln, bias=tinyb[:], scale=1024.0),
                    reads=[bps, B_w["tinyb"]], writes=[B_lns[li]])
                g.add("scalar", lambda e, c0=c0, li=li: e.activation(
                    rinvb[:, c0:c0 + 512], lns[li][:], AF.Exp, bias=0.0, scale=-0.5),
                    reads=[B_lns[li]], writes=[B_rinv[tc]])
            for d in range(DT // 2, DT):
                for s in range(4):
                    g.add("tensor", lambda e, d=d, s=s: e.matmul(
                        ps_o[:, s * 512:(s + 1) * 512], wdp[:, d * 128:(d + 1) * 128],
                        hs(d, s * 512, (s + 1) * 512),
                        start=False, stop=(d == DT - 1)),
                        reads=[B_hT[d], B_w["wdp"]], writes=[B_pso])
            for pt in range(2):
                g.add("vector", lambda e, pt=pt: e.tensor_tensor(
                    projT[:, pt * C:(pt + 1) * C], projT[:, pt * C:(pt + 1) * C],
                    rinvb[:, :], op=ALU.mult),
                    reads=[B_projT] + B_rinv, writes=[B_projT])
            for tt in range(16):
                slot = tt % 4
                pps = (ps_a, ps_a, ps_b, ps_b)[slot][:, (slot % 2) * 512:
                                                      (slot % 2) * 512 + 4]
                bps = (B_psa2[0], B_psa2[1], B_psb2[0], B_psb2[1])[slot]
                for pt in range(2):
                    g.add("tensor", lambda e, pps=pps, pt=pt, tt=tt: e.matmul(
                        pps, projT[:, pt * C + tt * 128:pt * C + (tt + 1) * 128],
                        simns[:, pt * 4:pt * 4 + 4], start=(pt == 0), stop=(pt == 1)),
                        reads=[B_projT, B_prep], writes=[bps])
                g.add("vector", lambda e, pps=pps, tt=tt: e.tensor_copy(
                    logit[:, tt * 4:(tt + 1) * 4], pps),
                    reads=[bps], writes=[B_logit])
            l3 = logit.rearrange("p (a b) -> p a b", b=4)
            maskv = lns[0][:, 0:64]
            evals = lns[0][:, 64:128]
            gates = lns[0][:, 128:192]
            g.add("vector", lambda e: e.tensor_reduce(lmax[:], l3, axis=AX, op=ALU.max),
                  reads=[B_logit], writes=[B_lmax])
            g.add("scalar", lambda e: e.activation(evals[:], logit[:], AF.Exp),
                  reads=[B_logit], writes=[B_ev])
            g.add("vector", lambda e, maskv=maskv: e.tensor_tensor(
                maskv.rearrange("p (a b) -> p a b", b=4), l3,
                lmax[:, :].unsqueeze(2).broadcast_to([128, 16, 4]), op=ALU.is_equal),
                reads=[B_logit, B_lmax], writes=[B_lns[0]])
            g.add("vector", lambda e: e.tensor_reduce(
                srec[:, :], evals.rearrange("p (a b) -> p a b", b=4), axis=AX,
                op=ALU.add),
                reads=[B_ev], writes=B_srec)
            g.add("vector", lambda e: e.reciprocal(ssum[:, :], srec[:, :]),
                  reads=B_srec, writes=B_ssum)
            g.add("vector", lambda e: e.tensor_tensor(
                l3, evals.rearrange("p (a b) -> p a b", b=4),
                ssum[:, :].unsqueeze(2).broadcast_to([128, 16, 4]), op=ALU.mult),
                reads=[B_ev] + B_ssum, writes=[B_logit])
            g.add("vector", lambda e, maskv=maskv: e.tensor_tensor(
                gates[:], logit[:], maskv, op=ALU.mult),
                reads=[B_logit, B_lns[0]], writes=[B_gates])
            for tt in range(16):
                ph = (ps_a if tt < 8 else ps_b)[0:4, (tt % 8) * 128:
                                                (tt % 8 + 1) * 128]
                bh = (B_psa2 if tt < 8 else B_psb2)[(tt % 8) // 4]
                g.add("tensor", lambda e, tt=tt, ph=ph: e.transpose(
                    ph, gates[:, tt * 4:(tt + 1) * 4], ident32[:]),
                    reads=[B_gates, B_w["ident32"]], writes=[bh])
            gatesT = vs[0][0:4, :]
            g.add("vector", lambda e, gatesT=gatesT: e.tensor_copy(
                gatesT[:, 0:1024], ps_a[0:4, :]),
                reads=B_psa2, writes=B_vsb)
            g.add("vector", lambda e, gatesT=gatesT: e.tensor_copy(
                gatesT[:, 1024:2048], ps_b[0:4, :]),
                reads=B_psb2, writes=B_vsb)
            gaterep = qdup[0]
            for s in range(4):
                pbh = ps_b[:, (s % 2) * 512:(s % 2) * 512 + 512]
                bh = B_psb2[s % 2]
                g.add("tensor", lambda e, s=s, pbh=pbh, gatesT=gatesT: e.matmul(
                    pbh, selm[:], gatesT[:, s * 512:(s + 1) * 512],
                    start=True, stop=True),
                    reads=B_vsb + [B_w["selm"]], writes=[bh])
                g.add("vector", lambda e, s=s, pbh=pbh, gaterep=gaterep: e.tensor_copy(
                    gaterep[:, s * 512:(s + 1) * 512], pbh),
                    reads=[bh], writes=[B_qdup[0]])
            downb = ET[0]
            g.add("vector", lambda e, downb=downb, gaterep=gaterep: e.tensor_tensor(
                downb[:], ps_o[:], gaterep[:], op=ALU.mult),
                reads=[B_pso, B_qdup[0]], writes=[B_ET[0]])
            pend_stats = []
            for tc in range(4):
                for ot in range(8):
                    slot = (tc * 8 + ot) % 4
                    pb = (ps_a, ps_a, ps_b, ps_b)[slot][:, (slot % 2) * 512:
                                                        (slot % 2) * 512 + 512]
                    bps = (B_psa2[0], B_psa2[1], B_psb2[0], B_psb2[1])[slot]
                    for d in range(DT):
                        g.add("tensor", lambda e, pb=pb, ot=ot, tc=tc, d=d: e.matmul(
                            pb, wwt[:, d * 1024 + ot * 128:d * 1024 + (ot + 1) * 128],
                            hs(d, tc * 512, (tc + 1) * 512),
                            start=(d == 0), stop=False),
                            reads=[B_hT[d], B_w["wwt"]], writes=[bps])
                    g.add("tensor", lambda e, pb=pb, ot=ot, tc=tc, downb=downb: e.matmul(
                        pb, wup[:, ot * 128:(ot + 1) * 128],
                        downb[:, tc * 512:(tc + 1) * 512],
                        start=False, stop=True),
                        reads=[B_ET[0], B_w["wup"]], writes=[bps])
                    g.add("vector", lambda e, pb=pb, ot=ot, tc=tc: e.tensor_tensor(
                        xs(ot, tc * 512, (tc + 1) * 512),
                        xs(ot, tc * 512, (tc + 1) * 512), pb, op=ALU.add),
                        reads=[bps, B_xT[ot][tc]], writes=[B_xT[ot][tc]])
                yv = dr[f"y{ib}"].ap()
                g.add("dma_s", lambda e, yv=yv, tc=tc: e.dma_start(
                    yv[:, tc * 4096:(tc + 1) * 4096],
                    xT[:, tc * 4096:(tc + 1) * 4096]),
                    reads=[B_xT[d][tc] for d in range(DT)])
                if ib + 1 < NB:
                    xv = dr[f"x{ib + 1}"].ap()
                    g.add("dma_s", lambda e, xv=xv, tc=tc: e.dma_start(
                        xT[:, tc * 4096:(tc + 1) * 4096],
                        xv[:, tc * 4096:(tc + 1) * 4096]),
                        writes=[B_xT[d][tc] for d in range(DT)])
                    pend_stats.append(tc)
                while pend_stats and tc - pend_stats[0] >= 2:
                    ln_stats(pend_stats.pop(0))
            for tcp in pend_stats:
                ln_stats(tcp)

        for ib in range(NB):
            if ib == 0:
                xv = dr["x0"].ap()
                for tc in range(4):
                    g.add("dma_s", lambda e, xv=xv, tc=tc: e.dma_start(
                        xT[:, tc * 4096:(tc + 1) * 4096],
                        xv[:, tc * 4096:(tc + 1) * 4096]),
                        writes=[B_xT[d][tc] for d in range(DT)])
                for tc in range(4):
                    ln_stats(tc)
            ln_apply()
            for h in range(H):
                attn_head(ib, h)
                if ib == 0 and h == 0:
                    prep_emit()
                if h % 2 == 1:
                    ln_stats(h // 2)
            ln_apply()
            ffn_phase(ib)

        if DEBUG_TAPS:
            for tap in DEBUG_TAPS.split(","):
                if tap == "hT":
                    dbg = nc.dram_tensor("dbg_hT", [128, DT * C], F32,
                                         kind="ExternalOutput")
                    g.add("dma_s", lambda e, dbg=dbg: e.dma_start(
                        dbg.ap(), hT[:]), reads=B_hT)

        g.finalize()

        sems = {}
        for chan in ENGS:
            sems[chan] = ctx.enter_context(nc.semaphore(f"sem_{chan}"))
        dma_sems = [t.sem for t in g.tasks if t.chan in DMAQ]
        for sname in dma_sems:
            sems[sname] = ctx.enter_context(nc.semaphore(f"sem_{sname}"))

        def epilogue(e):
            for sname in dma_sems:
                e.wait_ge(sems[sname], 16)
            return None
        t = Task(len(g.tasks), "sync", epilogue)
        t.sem = "sync"
        g.tasks.append(t)

        g.emit(nc, sems)
    return nc


_NC_CACHE = None


def kernel(**inputs):
    global _NC_CACHE
    in_maps = host_prep(inputs)
    if _NC_CACHE is None:
        _NC_CACHE = build_program()
    nc = _NC_CACHE
    res = run_bass_kernel_spmd(nc, in_maps, core_ids=list(range(NCORES)))
    out = np.empty((NCORES * NB, C, D), dtype=np.float32)
    for c in range(NCORES):
        for ib in range(NB):
            yb = res.results[c][f"y{ib}"]
            out[NB * c + ib] = yb.reshape(128, 4, 8, 512).transpose(
                1, 3, 2, 0).reshape(C, D)
    return out


# version 42 (final)
# speedup vs baseline: 1.1752x; 1.1752x over previous
import os
import numpy as np
import concourse.bass as bass
import concourse.mybir as mybir
from concourse.bass_utils import run_bass_kernel_spmd

F32 = mybir.dt.float32
BF16 = mybir.dt.bfloat16
AF = mybir.ActivationFunctionType
ALU = mybir.AluOpType
AX = mybir.AxisListType.X

NCORES = 8
NB = 2
C = 2048
D = 1024
H = 8
DT = 8
LN_EPS = 1e-5
SCALE = 0.125
CLAMP_MAX = float(np.log(1.0 / 0.01))

DEBUG_TAPS = os.environ.get("KMK_DEBUG_TAPS", "")
PH = int(os.environ.get("KMK_PH", "99"))


class Buf:
    __slots__ = ("name", "w", "readers")

    def __init__(self, name):
        self.name = name
        self.w = None
        self.readers = []


class Task:
    __slots__ = ("idx", "chan", "fn", "deps", "inc", "count", "sem")

    def __init__(self, idx, chan, fn):
        self.idx = idx
        self.chan = chan
        self.fn = fn
        self.deps = {}
        self.inc = False
        self.count = None
        self.sem = None


ENGS = ("tensor", "vector", "scalar", "gpsimd", "sync")
DMAQ = {"dma_s": "sync", "dma_g": "gpsimd", "dma_a": "scalar"}


def issue_engine(chan):
    return DMAQ.get(chan, chan)


class Graph:
    def __init__(self):
        self.tasks = []
        self.n_dma = 0

    def add(self, chan, fn, reads=(), writes=(), after=()):
        t = Task(len(self.tasks), chan, fn)
        if chan in DMAQ:
            t.sem = f"dma{self.n_dma}"
            self.n_dma += 1
        else:
            t.sem = chan
        self.tasks.append(t)
        for b in reads:
            if b.w is not None:
                t.deps[b.w] = 2
            b.readers.append(t.idx)
        for b in writes:
            if b.w is not None:
                t.deps[b.w] = max(t.deps.get(b.w, 0), 0)
            for r in b.readers:
                if r != t.idx:
                    t.deps[r] = max(t.deps.get(r, 0), 1)
            b.w = t.idx
            b.readers = []
        for p in after:
            t.deps[p] = 2
        return t.idx

    def finalize(self):
        for t in self.tasks:
            if t.chan in DMAQ:
                t.inc = True
        for t in self.tasks:
            for p, sev in t.deps.items():
                pt = self.tasks[p]
                if pt.chan in DMAQ:
                    need = True
                else:
                    need = sev >= 1 or (pt.chan != issue_engine(t.chan))
                if need:
                    pt.inc = True
        cnt = {}
        for t in self.tasks:
            if t.inc:
                step = 16 if t.chan in DMAQ else 1
                cnt[t.sem] = cnt.get(t.sem, 0) + step
                t.count = cnt[t.sem]
        self.final_counts = cnt

    def emit(self, nc, sems):
        by_eng = {e: [] for e in ENGS}
        for t in self.tasks:
            by_eng[issue_engine(t.chan)].append(t)
        watermark = {}

        def emit_engine(eng_name, eng):
            for t in by_eng[eng_name]:
                req = {}
                for p, sev in t.deps.items():
                    pt = self.tasks[p]
                    if pt.chan in DMAQ:
                        need = True
                    else:
                        need = sev >= 1 or (pt.chan != eng_name)
                    if not need:
                        continue
                    assert pt.inc and pt.count is not None
                    req[pt.sem] = max(req.get(pt.sem, 0), pt.count)
                for chan in sorted(req):
                    v = req[chan]
                    key = (eng_name, chan)
                    if watermark.get(key, 0) >= v:
                        continue
                    eng.wait_ge(sems[chan], v)
                    watermark[key] = v
                inst = t.fn(eng)
                if t.inc:
                    assert inst is not None, f"task {t.idx} on {t.chan} has no inst"
                    step = 16 if t.chan in DMAQ else 1
                    inst.then_inc(sems[t.sem], step)

        with nc.Block() as blk:
            blk.tensor(lambda e: emit_engine("tensor", e))
            blk.vector(lambda e: emit_engine("vector", e))
            blk.scalar(lambda e: emit_engine("scalar", e))
            blk.gpsimd(lambda e: emit_engine("gpsimd", e))
            blk.sync(lambda e: emit_engine("sync", e))


def _wtile(a, free):
    return np.ascontiguousarray(
        a.reshape(DT, 128, free).transpose(1, 0, 2).reshape(128, DT * free))


def host_prep(inp):
    x = inp["L0_attn_x"]
    w = {}
    mq = inp["L0_attn_Mq"]
    mqd = np.ascontiguousarray(np.broadcast_to(
        mq.reshape(D, 8, 1, 64), (D, 8, 2, 64)).reshape(D, 1024))
    w["wqd"] = _wtile(mqd, 1024)
    w["wk"] = _wtile(inp["L0_attn_Mk"], 512)
    w["wv"] = _wtile(inp["L0_attn_Mv"], 1024)
    w["wwt"] = _wtile(np.ascontiguousarray(inp["L0_ffn_W"].T), 1024)
    w["wp"] = _wtile(np.ascontiguousarray(inp["L0_ffn_Wp"].T), 256)
    ranks = (1, 2, 4, 8)
    wdpad = np.zeros((128, D), dtype=np.float32)
    wupad = np.zeros((D, 128), dtype=np.float32)
    for e, r in enumerate(ranks):
        wdpad[32 * e:32 * e + r, :] = inp[f"L0_ffn_Wd{e}"]
        wupad[:, 32 * e:32 * e + r] = inp[f"L0_ffn_Wu{e}"]
    w["wdp"] = _wtile(np.ascontiguousarray(wdpad.T), 128)
    w["wup"] = np.ascontiguousarray(wupad.T)
    w["simw"] = np.ascontiguousarray(
        inp["L0_ffn_sim"].reshape(2, 128, 4).transpose(1, 0, 2).reshape(128, 8))
    w["bpw"] = np.ascontiguousarray(inp["L0_ffn_bp"].reshape(2, 128).T)
    w["tempw"] = np.ascontiguousarray(inp["L0_ffn_temp"].reshape(1, 1))
    w["onesm"] = np.full((128, 128), 1.0 / 1024.0, dtype=np.float32)

    w["selm"] = np.ascontiguousarray(
        (np.arange(4)[:, None] == (np.arange(128)[None, :] // 32)).astype(np.float32))
    w["identc"] = np.eye(128, dtype=np.float32)
    w["onesrow"] = np.ones((1, 128), dtype=np.float32)

    in_maps = []
    for c in range(NCORES):
        m = dict(w)
        for ib in range(NB):
            xb = x[NB * c + ib]
            m[f"x{ib}"] = np.ascontiguousarray(
                xb.reshape(4, 512, 8, 128).transpose(3, 0, 2, 1).reshape(128, 16384))
        in_maps.append(m)
    return in_maps


def build_program():
    nc = bass.Bass("TRN2", target_bir_lowering=False)

    dr = {}
    for ib in range(NB):
        dr[f"x{ib}"] = nc.dram_tensor(f"x{ib}", [128, DT * C], F32,
                                      kind="ExternalInput")
        dr[f"y{ib}"] = nc.dram_tensor(f"y{ib}", [128, DT * C], F32,
                                      kind="ExternalOutput")
    for name, shape in [
        ("wqd", [128, DT * 1024]), ("wk", [128, DT * 512]), ("wv", [128, DT * 1024]),
        ("wwt", [128, DT * 1024]), ("wp", [128, DT * 256]), ("wdp", [128, DT * 128]),
        ("wup", [128, 1024]), ("simw", [128, 8]), ("bpw", [128, 2]),
        ("tempw", [1, 1]), ("onesm", [128, 128]), ("selm", [4, 128]),
        ("identc", [128, 128]), ("onesrow", [1, 128]),
    ]:
        dr[name] = nc.dram_tensor(name, shape, F32, kind="ExternalInput")

    taps = []
    from contextlib import ExitStack
    ctx = ExitStack()
    sb = {}

    def S(name, shape, dtype):
        sb[name] = ctx.enter_context(nc.sbuf_tensor("s_" + name, shape, dtype))
        return sb[name]

    with ctx:
        xT = S("xT", [128, DT * C], F32)
        hT = S("hT", [128, DT * C], BF16)
        wqd = S("wqd", [128, DT * 1024], BF16)
        wk = S("wk", [128, DT * 512], BF16)
        wv = S("wv", [128, DT * 1024], BF16)
        wwt = S("wwt", [128, DT * 1024], BF16)
        wp = S("wp", [128, DT * 256], BF16)
        wdp = S("wdp", [128, DT * 128], BF16)
        wup = S("wup", [128, 1024], BF16)
        simb = S("simb", [128, 8], BF16)
        bpb = S("bpb", [128, 2], F32)
        tempb = S("tempb", [1, 1], F32)
        onesm = S("onesm", [128, 128], BF16)
        onesm32 = S("onesm32", [128, 128], F32)
        selm = S("selm", [4, 128], BF16)
        identb = S("identb", [128, 128], BF16)
        ident32 = S("ident32", [128, 128], F32)
        onesrow = S("onesrow", [1, 128], BF16)
        xsq = [S(f"xsq{i}", [128, 512], BF16) for i in range(2)]
        lns = [S("lns0", [128, 512], F32)]
        epsb = S("epsb", [128, 1], F32)
        tinyb = S("tinyb", [128, 1], F32)
        kzp = [[S("kzE0", [128, 1024], BF16), S("kzO0", [128, 1024], BF16)]]
        m_b = S("m_b", [128, C], BF16)
        rinvb = S("rinvb", [128, C], BF16)
        qdup = [S("qdup0", [128, 2048], BF16)]
        ET = [S(f"ET{i}", [128, 2048], BF16) for i in range(4)]
        spart = S("spart", [128, 32], F32)
        ssum = S("ssum", [128, 16], F32)
        srec = S("srec", [128, 16], F32)
        projT = S("projT", [128, 2 * C], BF16)
        vh = [projT[:, 0:2048]]
        vs = [projT[:, 2048:4096]]
        sqp = [S("sqp0", [128, 512], BF16)]
        logit = S("logit", [128, 64], F32)
        sq8 = S("sq8", [128, 8], BF16)
        lmax = lns[0][:, 192:208]
        lnsim = lns[0][:, 208:212]
        tmin = lns[0][0:1, 212:213]
        texp = lns[0][0:1, 213:214]
        screp = lns[0][:, 214:215]
        rsimb = S("rsimb", [128, 4], BF16)
        texb = S("texb", [1, 1], BF16)
        simns = S("simns", [128, 8], BF16)

        ps_a = ctx.enter_context(nc.psum_tensor("ps_a", [128, 1024], F32))
        ps_b = ctx.enter_context(nc.psum_tensor("ps_b", [128, 1024], F32))
        ps_o = ctx.enter_context(nc.psum_tensor("ps_o", [128, 2048], F32))

        g = Graph()

        B_xT = [[Buf(f"xT{d}_{t}") for t in range(4)] for d in range(DT)]
        B_hT = [Buf(f"hT{d}") for d in range(DT)]
        B_w = {n: Buf(n) for n in ["wqd", "wk", "wv", "wwt", "wp", "wdp", "wup",
                                   "simb", "bpb", "tempb", "onesm", "onesm32",
                                   "selm", "identb", "onesrow", "epsb",
                                   "tinyb", "ident32"]}
        B_xsq = [Buf(f"xsq{i}") for i in range(2)]
        B_mb = [Buf("m_b_et1")] * 4
        B_rinv = [Buf("rinv_et2")] * 4
        B_lns = [Buf("lns0")]
        B_psa2 = [Buf("ps_a0"), Buf("ps_a1")]
        B_psb2 = [Buf("ps_b0"), Buf("ps_b1")]
        B_psa = B_psa2
        B_psb = B_psb2
        B_pso = Buf("ps_o")
        B_kz4 = [[Buf(f"kz{par}_{ot}") for ot in range(4)] for par in range(2)]
        B_qdup2 = [Buf("qdup_h0"), Buf("qdup_h1")]
        B_qdup = [B_qdup2[0]]
        B_vsb = [Buf(f"vs_{b}") for b in range(8)]
        B_ET = [Buf(f"ET{i}") for i in range(4)]
        B_ET[1] = B_mb[0]
        B_ET[2] = B_rinv[0]

        B_sp = [Buf(f"spart{j}") for j in range(16)]
        B_ssum = [Buf(f"ssum{b}") for b in range(8)]
        B_srec = [Buf(f"srec{b}") for b in range(8)]
        B_projT = Buf("projT")
        B_vh = [B_projT]
        B_sqp = [Buf("sqp0")]
        B_logit = Buf("logit")
        B_ev = B_lns[0]
        B_lmax = Buf("lmax")
        B_gates = B_lns[0]
        B_prep = Buf("simprep")

        def dma_cast(name_d, dst, bufs):
            g.add("dma_g", lambda e, s=dst, dn=name_d: e.dma_start(s[:], dr[dn].ap()),
                  writes=bufs)

        dma_cast("onesm", onesm, [B_w["onesm"]])
        dma_cast("wqd", wqd, [B_w["wqd"]])
        dma_cast("wk", wk, [B_w["wk"]])
        dma_cast("wv", wv, [B_w["wv"]])
        dma_cast("wwt", wwt, [B_w["wwt"]])
        dma_cast("wp", wp, [B_w["wp"]])
        dma_cast("wdp", wdp, [B_w["wdp"]])
        dma_cast("wup", wup, [B_w["wup"]])
        dma_cast("simw", simb, [B_w["simb"]])
        dma_cast("selm", selm, [B_w["selm"]])
        dma_cast("identc", identb, [B_w["identb"]])
        dma_cast("onesrow", onesrow, [B_w["onesrow"]])
        g.add("dma_s", lambda e: e.dma_start(bpb[:], dr["bpw"].ap()), writes=[B_w["bpb"]])
        g.add("dma_s", lambda e: e.dma_start(onesm32[:], dr["onesm"].ap()),
              writes=[B_w["onesm32"]])
        g.add("dma_s", lambda e: e.dma_start(ident32[:], dr["identc"].ap()),
              writes=[B_w["ident32"]])
        g.add("dma_s", lambda e: e.dma_start(tempb[:], dr["tempw"].ap()),
              writes=[B_w["tempb"]])
        g.add("vector", lambda e: e.memset(epsb[:], LN_EPS), writes=[B_w["epsb"]])
        g.add("vector", lambda e: e.memset(kzp[0][0][64:128, :], 0.0),
              writes=B_kz4[0])
        g.add("vector", lambda e: e.memset(kzp[0][1][0:64, :], 0.0),
              writes=B_kz4[1])
        g.add("vector", lambda e: e.memset(tinyb[:], 1e-30), writes=[B_w["tinyb"]])

        def prep_emit():
            g.add("vector", lambda e: e.tensor_tensor(sq8[:], simb[:], simb[:], op=ALU.mult),
                  reads=[B_w["simb"]], writes=[B_prep])
            g.add("tensor", lambda e: e.matmul(ps_a[:, 0:4], onesm[:], sq8[:, 0:4],
                                               start=True, stop=False),
                  reads=[B_prep, B_w["onesm"]], writes=[B_psa2[0]])
            g.add("tensor", lambda e: e.matmul(ps_a[:, 0:4], onesm[:], sq8[:, 4:8],
                                               start=False, stop=True),
                  reads=[B_prep, B_w["onesm"]], writes=[B_psa2[0]])
            g.add("scalar", lambda e: e.activation(lnsim[:], ps_a[:, 0:4], AF.Ln,
                                                   bias=tinyb[:], scale=1024.0),
                  reads=[B_psa2[0], B_w["tinyb"]], writes=[B_prep])
            g.add("scalar", lambda e: e.activation(rsimb[:], lnsim[:], AF.Exp,
                                                   bias=0.0, scale=-0.5),
                  reads=[B_prep], writes=[B_prep])
            g.add("vector", lambda e: e.tensor_scalar(tmin[:], tempb[:], CLAMP_MAX, None,
                                                      op0=ALU.min),
                  reads=[B_w["tempb"]], writes=[B_prep])
            g.add("scalar", lambda e: e.activation(texp[:], tmin[:], AF.Exp),
                  reads=[B_prep], writes=[B_prep])
            g.add("vector", lambda e: e.tensor_copy(texb[:], texp[:]),
                  reads=[B_prep], writes=[B_prep])
            g.add("tensor", lambda e: e.matmul(ps_b[:, 0:1], onesrow[:], texb[:],
                                               start=True, stop=True),
                  reads=[B_prep, B_w["onesrow"]], writes=[B_psb2[0]])
            g.add("vector", lambda e: e.tensor_copy(screp[:], ps_b[:, 0:1]),
                  reads=[B_psb2[0]], writes=[B_prep])
            for pt in range(2):
                g.add("vector", lambda e, pt=pt: e.tensor_tensor(
                    simns[:, 4 * pt:4 * pt + 4], simb[:, 4 * pt:4 * pt + 4], rsimb[:],
                    op=ALU.mult),
                    reads=[B_w["simb"], B_prep], writes=[B_prep])
                g.add("vector", lambda e, pt=pt: e.tensor_scalar(
                    simns[:, 4 * pt:4 * pt + 4], simns[:, 4 * pt:4 * pt + 4], screp[:],
                    None, op0=ALU.mult),
                    reads=[B_prep], writes=[B_prep])

        def xs(d, c0, c1):
            tc = c0 // 512
            assert (c1 - 1) // 512 == tc
            base = tc * 4096 + d * 512 + (c0 % 512)
            return xT[:, base:base + (c1 - c0)]

        def hs(d, c0, c1):
            return hT[:, d * C + c0:d * C + c1]

        def ln_stats(tc, sq_act=False):
            c0 = tc * 512
            mp = (ps_a if tc % 2 == 0 else ps_b)[:, 0:512]
            sp = (ps_a if tc % 2 == 0 else ps_b)[:, 512:1024]
            bpsl = B_psa2 if tc % 2 == 0 else B_psb2
            for d in range(DT):
                i = d % 2
                if sq_act:
                    g.add("scalar", lambda e, d=d, c0=c0, i=i: e.activation(
                        xsq[i][:], xs(d, c0, c0 + 512), AF.Square),
                        reads=[B_xT[d][tc]], writes=[B_xsq[i]])
                else:
                    g.add("gpsimd", lambda e, d=d, c0=c0, i=i: e.tensor_tensor(
                        xsq[i][:], xs(d, c0, c0 + 512), xs(d, c0, c0 + 512),
                        op=ALU.mult),
                        reads=[B_xT[d][tc]], writes=[B_xsq[i]])
                g.add("tensor", lambda e, mp=mp, d=d, c0=c0: e.matmul(
                    mp, onesm32[:], xs(d, c0, c0 + 512),
                    start=(d == 0), stop=(d == DT - 1)),
                    reads=[B_xT[d][tc], B_w["onesm32"]], writes=[bpsl[0]])
                g.add("tensor", lambda e, sp=sp, i=i, d=d: e.matmul(
                    sp, onesm[:], xsq[i][:], start=(d == 0), stop=(d == DT - 1)),
                    reads=[B_xsq[i], B_w["onesm"]], writes=[bpsl[1]])
            li = 0
            g.add("vector", lambda e, mp=mp, c0=c0: e.tensor_copy(
                m_b[:, c0:c0 + 512], mp),
                reads=[bpsl[0]], writes=[B_mb[tc]])
            g.add("vector", lambda e, c0=c0, li=li: e.tensor_tensor(
                lns[li][:], m_b[:, c0:c0 + 512], m_b[:, c0:c0 + 512], op=ALU.mult),
                reads=[B_mb[tc]], writes=[B_lns[li]])
            g.add("vector", lambda e, sp=sp, li=li: e.tensor_tensor(
                lns[li][:], sp, lns[li][:], op=ALU.subtract),
                reads=[bpsl[1], B_lns[li]], writes=[B_lns[li]])
            g.add("scalar", lambda e, li=li: e.activation(
                lns[li][:], lns[li][:], AF.Ln, bias=epsb[:], scale=1.0),
                reads=[B_lns[li], B_w["epsb"]], writes=[B_lns[li]])
            g.add("scalar", lambda e, c0=c0, li=li: e.activation(
                rinvb[:, c0:c0 + 512], lns[li][:], AF.Exp, bias=0.0, scale=-0.5),
                reads=[B_lns[li]], writes=[B_rinv[tc]])
        def ln_apply():
            for d in range(DT):
                hv = bass.AP(hT, d * C, [[DT * C, 128], [512, 4], [1, 512]])
                xv = bass.AP(xT, d * 512, [[DT * C, 128], [4096, 4], [1, 512]])
                mv = bass.AP(m_b, 0, [[C, 128], [512, 4], [1, 512]])
                rv = bass.AP(rinvb, 0, [[C, 128], [512, 4], [1, 512]])
                g.add("vector", lambda e, hv=hv, xv=xv, mv=mv: e.tensor_tensor(
                    hv, xv, mv, op=ALU.subtract),
                    reads=B_xT[d] + B_mb, writes=[B_hT[d]])
                g.add("vector", lambda e, d=d: e.tensor_tensor(
                    hs(d, 0, C), hs(d, 0, C), rinvb[:, :], op=ALU.mult),
                    reads=[B_hT[d]] + B_rinv, writes=[B_hT[d]])

        def ln_phase(ib):
            for tc in range(4):
                ln_stats(tc)
            ln_apply()

        def attn_head(ib, h):
            hb = 0
            c0 = h * 256
            vunits = []

            def qd_half(half, douter):
                if douter:
                    for d in range(DT):
                        for b4 in range(4):
                            b = half * 4 + b4
                            g.add("tensor", lambda e, b=b, b4=b4, d=d: e.matmul(
                                ps_a[:, b4 * 256:(b4 + 1) * 256],
                                wqd[:, d * 1024 + b * 128:d * 1024 + (b + 1) * 128],
                                hs(d, c0, c0 + 256),
                                start=(d == 0 and b4 % 2 == 0),
                                stop=(d == DT - 1 and b4 % 2 == 1)),
                                reads=[B_hT[d], B_w["wqd"]], writes=[B_psa2[b4 // 2]])
                else:
                    for b4 in range(4):
                        b = half * 4 + b4
                        for d in range(DT):
                            g.add("tensor", lambda e, b=b, b4=b4, d=d: e.matmul(
                                ps_a[:, b4 * 256:(b4 + 1) * 256],
                                wqd[:, d * 1024 + b * 128:d * 1024 + (b + 1) * 128],
                                hs(d, c0, c0 + 256),
                                start=(d == 0), stop=(d == DT - 1)),
                                reads=[B_hT[d], B_w["wqd"]], writes=[B_psa2[b4 // 2]])
                g.add("vector", lambda e, half=half: e.tensor_copy(
                    qdup[0][:, half * 1024:(half + 1) * 1024], ps_a[:]),
                    reads=B_psa2, writes=[B_qdup2[half]])

            def k_all(douter):
                if douter:
                    for d in range(DT):
                        for ot in range(4):
                            g.add("tensor", lambda e, ot=ot, d=d: e.matmul(
                                ps_b[:, ot * 256:(ot + 1) * 256],
                                wk[:, d * 512 + ot * 128:d * 512 + (ot + 1) * 128],
                                hs(d, c0, c0 + 256),
                                start=(d == 0 and ot % 2 == 0),
                                stop=(d == DT - 1 and ot % 2 == 1)),
                                reads=[B_hT[d], B_w["wk"]], writes=[B_psb2[ot // 2]])
                else:
                    for ot in range(4):
                        for d in range(DT):
                            g.add("tensor", lambda e, ot=ot, d=d: e.matmul(
                                ps_b[:, ot * 256:(ot + 1) * 256],
                                wk[:, d * 512 + ot * 128:d * 512 + (ot + 1) * 128],
                                hs(d, c0, c0 + 256),
                                start=(d == 0), stop=(d == DT - 1)),
                                reads=[B_hT[d], B_w["wk"]], writes=[B_psb2[ot // 2]])
                g.add("vector", lambda e: e.tensor_copy(
                    kzp[0][0][0:64, :], ps_b[0:64, :]),
                    reads=B_psb2, writes=list(B_kz4[0]))
                g.add("vector", lambda e: e.tensor_copy(
                    kzp[0][1][64:128, :], ps_b[64:128, :]),
                    reads=B_psb2, writes=list(B_kz4[1]))

            first = (h == 0)
            if first:
                for d in range(DT):
                    for b4 in range(4):
                        g.add("tensor", lambda e, b4=b4, d=d: e.matmul(
                            ps_a[:, b4 * 256:(b4 + 1) * 256],
                            wqd[:, d * 1024 + b4 * 128:d * 1024 + (b4 + 1) * 128],
                            hs(d, c0, c0 + 256),
                            start=(d == 0 and b4 % 2 == 0),
                            stop=(d == DT - 1 and b4 % 2 == 1)),
                            reads=[B_hT[d], B_w["wqd"]], writes=[B_psa2[b4 // 2]])
                    for ot in range(4):
                        g.add("tensor", lambda e, ot=ot, d=d: e.matmul(
                            ps_b[:, ot * 256:(ot + 1) * 256],
                            wk[:, d * 512 + ot * 128:d * 512 + (ot + 1) * 128],
                            hs(d, c0, c0 + 256),
                            start=(d == 0 and ot % 2 == 0),
                            stop=(d == DT - 1 and ot % 2 == 1)),
                            reads=[B_hT[d], B_w["wk"]], writes=[B_psb2[ot // 2]])
                    for s4 in range(4):
                        tt, s = s4 // 2, s4 % 2
                        g.add("tensor", lambda e, tt=tt, d=d, s=s: e.matmul(
                            ps_o[:, tt * 1024 + s * 512:tt * 1024 + (s + 1) * 512],
                            hs(d, c0 + tt * 128, c0 + (tt + 1) * 128),
                            wv[:, d * 1024 + s * 512:d * 1024 + (s + 1) * 512],
                            start=(d == 0), stop=(d == DT - 1)),
                            reads=[B_hT[d], B_w["wv"]], writes=[B_pso])
                g.add("vector", lambda e: e.tensor_copy(
                    qdup[0][:, 0:1024], ps_a[:]),
                    reads=B_psa2, writes=[B_qdup2[0]])
                g.add("vector", lambda e: e.tensor_copy(
                    kzp[0][0][0:64, :], ps_b[0:64, :]),
                    reads=B_psb2, writes=list(B_kz4[0]))
                g.add("vector", lambda e: e.tensor_copy(
                    kzp[0][1][64:128, :], ps_b[64:128, :]),
                    reads=B_psb2, writes=list(B_kz4[1]))
                g.add("vector", lambda e: e.tensor_copy(vh[hb][:], ps_o[:]),
                      reads=[B_pso], writes=[B_projT])
                qd_half(1, False)
            else:
                qd_half(0, False)
                k_all(False)
                qd_half(1, False)
                for tt in range(2):
                    for s in range(2):
                        unit = []
                        for d in range(DT):
                            unit.append(lambda tt=tt, d=d, s=s: g.add(
                                "tensor", lambda e, tt=tt, d=d, s=s: e.matmul(
                                    ps_o[:, tt * 1024 + s * 512:
                                         tt * 1024 + (s + 1) * 512],
                                    hs(d, c0 + tt * 128, c0 + (tt + 1) * 128),
                                    wv[:, d * 1024 + s * 512:
                                       d * 1024 + (s + 1) * 512],
                                    start=(d == 0), stop=(d == DT - 1)),
                                reads=[B_hT[d], B_w["wv"]], writes=[B_pso]))
                        if s == 1:
                            unit.append(lambda tt=tt: g.add(
                                "vector", lambda e, tt=tt: e.tensor_copy(
                                    vh[hb][:, tt * 1024:(tt + 1) * 1024],
                                    ps_o[:, tt * 1024:(tt + 1) * 1024]),
                                reads=[B_pso], writes=[B_projT]))
                        vunits.append(unit)
            pend = []

            def push_out(bj):
                for j2 in (2 * bj, 2 * bj + 1):
                    vcol = (j2 % 2) * 1024 + (j2 // 2) * 128
                    for s in range(4):
                        pend.append((j2, lambda j2=j2, vcol=vcol, s=s: g.add(
                            "tensor", lambda e, j2=j2, vcol=vcol, s=s: e.matmul(
                                ps_o[:, s * 512:(s + 1) * 512],
                                vs[0][:, vcol:vcol + 128],
                                ET[j2 % 4][:, s * 512:(s + 1) * 512],
                                start=(j2 == 0), stop=(j2 == 15)),
                            reads=[B_vsb[j2 // 2], B_ET[j2 % 4]], writes=[B_pso])))

            def drain_out(n):
                for _ in range(min(n, len(pend))):
                    pend.pop(0)[1]()

            for jt in range(16):
                while pend and pend[0][0] <= jt - 4:
                    pend.pop(0)[1]()
                pj = (jt // 2) % 2
                lhs_c = (jt // 4) * 256 + (jt % 2) * 128
                for ihf in range(2):
                    pbuf = (ps_a, ps_b)[(2 * jt + ihf) % 2]
                    bl = (B_psa2, B_psb2)[(2 * jt + ihf) % 2]
                    for half in range(2):
                        g.add("tensor", lambda e, pbuf=pbuf, pj=pj, lhs_c=lhs_c,
                              half=half, ihf=ihf: e.matmul(
                                  pbuf[:, half * 512:(half + 1) * 512],
                                  kzp[0][pj][:, lhs_c:lhs_c + 128],
                                  qdup[0][:, ihf * 1024 + half * 512:
                                       ihf * 1024 + (half + 1) * 512],
                                  start=True, stop=True),
                              reads=[B_kz4[pj][jt // 4], B_qdup2[ihf]],
                              writes=[bl[half]])
                    g.add("scalar", lambda e, pbuf=pbuf, jt=jt, ihf=ihf: e.activation(
                        ET[jt % 4][:, ihf * 1024:(ihf + 1) * 1024], pbuf[:], AF.Exp,
                        bias=0.0, scale=SCALE,
                        accum_out=spart[:, 2 * jt + ihf:2 * jt + ihf + 1]),
                        reads=bl, writes=[B_ET[jt % 4], B_sp[jt]])
                    if vunits:
                        for th in vunits.pop(0):
                            th()
                    else:
                        ndr = 0
                        while pend and len(pend) > 16 and ndr < 2:
                            pend.pop(0)[1]()
                            ndr += 1
                if jt % 2 == 1:
                    bj = jt // 2
                    g.add("vector", lambda e, bj=bj: e.tensor_tensor(
                        ssum[:, 2 * bj:2 * bj + 2],
                        spart[:, 4 * bj:4 * bj + 4:2],
                        spart[:, 4 * bj + 1:4 * bj + 4:2], op=ALU.add),
                        reads=[B_sp[jt - 1], B_sp[jt]], writes=[B_ssum[bj]])
                    g.add("vector", lambda e, bj=bj: e.reciprocal(
                        srec[:, 2 * bj:2 * bj + 2], ssum[:, 2 * bj:2 * bj + 2]),
                        reads=[B_ssum[bj]], writes=[B_srec[bj]])
                    g.add("vector", lambda e, bj=bj: e.tensor_tensor(
                        bass.AP(sb["projT"], 2048 + bj * 128,
                                [[2 * C, 128], [1024, 2], [1, 128]]),
                        bass.AP(sb["projT"], bj * 128,
                                [[2 * C, 128], [1024, 2], [1, 128]]),
                        srec[:, 2 * bj:2 * bj + 2].unsqueeze(2)
                        .broadcast_to([128, 2, 128]), op=ALU.mult),
                        reads=[B_projT, B_srec[bj]], writes=[B_vsb[bj]])
                    push_out(bj)
            drain_out(len(pend))
            xview = bass.AP(xT, (h // 2) * 4096 + (h % 2) * 256,
                            [[DT * C, 128], [512, 8], [1, 256]])
            oview = bass.AP(ps_o, 0, [[2048, 128], [256, 8], [1, 256]])
            xbufs = [B_xT[b][h // 2] for b in range(8)]
            g.add("vector", lambda e, xview=xview, oview=oview: e.tensor_tensor(
                xview, xview, oview, op=ALU.add),
                reads=[B_pso] + xbufs, writes=xbufs)

        def ffn_phase(ib):
            def p1ps(s):
                return (ps_a, ps_a, ps_b, ps_b)[s][:, (s % 2) * 512:
                                                   (s % 2) * 512 + 512]

            def p1buf(s):
                return (B_psa2[0], B_psa2[1], B_psb2[0], B_psb2[1])[s]
            for d in range(DT):
                for s in range(4):
                    g.add("tensor", lambda e, d=d, s=s: e.matmul(
                        ps_o[:, s * 512:(s + 1) * 512],
                        wp[:, d * 256:d * 256 + 128],
                        hs(d, s * 512, (s + 1) * 512),
                        start=(d == 0), stop=(d == DT - 1)),
                        reads=[B_hT[d], B_w["wp"]], writes=[B_pso])
                for s in range(4):
                    g.add("tensor", lambda e, d=d, s=s: e.matmul(
                        p1ps(s),
                        wp[:, d * 256 + 128:d * 256 + 256],
                        hs(d, s * 512, (s + 1) * 512),
                        start=(d == 0), stop=(d == DT - 1)),
                        reads=[B_hT[d], B_w["wp"]], writes=[p1buf(s)])
            g.add("vector", lambda e: e.tensor_scalar(
                projT[:, 0:C], ps_o[:], bpb[:, 0:1], None, op0=ALU.add),
                reads=[B_pso, B_w["bpb"]], writes=[B_projT] + B_vsb)
            g.add("vector", lambda e: e.tensor_scalar(
                projT[:, C:C + 1024], ps_a[:], bpb[:, 1:2], None, op0=ALU.add),
                reads=B_psa2 + [B_w["bpb"]], writes=[B_projT])
            g.add("vector", lambda e: e.tensor_scalar(
                projT[:, C + 1024:2 * C], ps_b[:], bpb[:, 1:2], None, op0=ALU.add),
                reads=B_psb2 + [B_w["bpb"]], writes=[B_projT])
            for d in range(DT // 2):
                for s in range(4):
                    g.add("tensor", lambda e, d=d, s=s: e.matmul(
                        ps_o[:, s * 512:(s + 1) * 512], wdp[:, d * 128:(d + 1) * 128],
                        hs(d, s * 512, (s + 1) * 512),
                        start=(d == 0), stop=False),
                        reads=[B_hT[d], B_w["wdp"]], writes=[B_pso])
            for tc in range(4):
                c0 = tc * 512
                nps = (ps_a if tc % 2 == 0 else ps_b)[:, 0:512]
                bps = B_psa2[0] if tc % 2 == 0 else B_psb2[0]
                for pt in range(2):
                    i = 0
                    g.add("gpsimd", lambda e, pt=pt, c0=c0, i=i: e.tensor_tensor(
                        sqp[i][:], projT[:, pt * C + c0:pt * C + c0 + 512],
                        projT[:, pt * C + c0:pt * C + c0 + 512], op=ALU.mult),
                        reads=[B_projT], writes=[B_sqp[i]])
                    g.add("tensor", lambda e, nps=nps, i=i, pt=pt: e.matmul(
                        nps, onesm[:], sqp[i][:], start=(pt == 0), stop=(pt == 1)),
                        reads=[B_sqp[i], B_w["onesm"]], writes=[bps])
                li = 0
                g.add("scalar", lambda e, nps=nps, li=li: e.activation(
                    lns[li][:], nps, AF.Ln, bias=tinyb[:], scale=1024.0),
                    reads=[bps, B_w["tinyb"]], writes=[B_lns[li]])
                g.add("scalar", lambda e, c0=c0, li=li: e.activation(
                    rinvb[:, c0:c0 + 512], lns[li][:], AF.Exp, bias=0.0, scale=-0.5),
                    reads=[B_lns[li]], writes=[B_rinv[tc]])
            for d in range(DT // 2, DT):
                for s in range(4):
                    g.add("tensor", lambda e, d=d, s=s: e.matmul(
                        ps_o[:, s * 512:(s + 1) * 512], wdp[:, d * 128:(d + 1) * 128],
                        hs(d, s * 512, (s + 1) * 512),
                        start=False, stop=(d == DT - 1)),
                        reads=[B_hT[d], B_w["wdp"]], writes=[B_pso])
            for pt in range(2):
                g.add("vector", lambda e, pt=pt: e.tensor_tensor(
                    projT[:, pt * C:(pt + 1) * C], projT[:, pt * C:(pt + 1) * C],
                    rinvb[:, :], op=ALU.mult),
                    reads=[B_projT] + B_rinv, writes=[B_projT])
            for tt in range(16):
                slot = tt % 4
                pps = (ps_a, ps_a, ps_b, ps_b)[slot][:, (slot % 2) * 512:
                                                      (slot % 2) * 512 + 4]
                bps = (B_psa2[0], B_psa2[1], B_psb2[0], B_psb2[1])[slot]
                for pt in range(2):
                    g.add("tensor", lambda e, pps=pps, pt=pt, tt=tt: e.matmul(
                        pps, projT[:, pt * C + tt * 128:pt * C + (tt + 1) * 128],
                        simns[:, pt * 4:pt * 4 + 4], start=(pt == 0), stop=(pt == 1)),
                        reads=[B_projT, B_prep], writes=[bps])
                g.add("vector", lambda e, pps=pps, tt=tt: e.tensor_copy(
                    logit[:, tt * 4:(tt + 1) * 4], pps),
                    reads=[bps], writes=[B_logit])
            l3 = logit.rearrange("p (a b) -> p a b", b=4)
            maskv = lns[0][:, 0:64]
            evals = lns[0][:, 64:128]
            gates = lns[0][:, 128:192]
            g.add("vector", lambda e: e.tensor_reduce(lmax[:], l3, axis=AX, op=ALU.max),
                  reads=[B_logit], writes=[B_lmax])
            g.add("scalar", lambda e: e.activation(evals[:], logit[:], AF.Exp),
                  reads=[B_logit], writes=[B_ev])
            g.add("vector", lambda e, maskv=maskv: e.tensor_tensor(
                maskv.rearrange("p (a b) -> p a b", b=4), l3,
                lmax[:, :].unsqueeze(2).broadcast_to([128, 16, 4]), op=ALU.is_equal),
                reads=[B_logit, B_lmax], writes=[B_lns[0]])
            g.add("vector", lambda e: e.tensor_reduce(
                srec[:, :], evals.rearrange("p (a b) -> p a b", b=4), axis=AX,
                op=ALU.add),
                reads=[B_ev], writes=B_srec)
            g.add("vector", lambda e: e.reciprocal(ssum[:, :], srec[:, :]),
                  reads=B_srec, writes=B_ssum)
            g.add("vector", lambda e: e.tensor_tensor(
                l3, evals.rearrange("p (a b) -> p a b", b=4),
                ssum[:, :].unsqueeze(2).broadcast_to([128, 16, 4]), op=ALU.mult),
                reads=[B_ev] + B_ssum, writes=[B_logit])
            g.add("vector", lambda e, maskv=maskv: e.tensor_tensor(
                gates[:], logit[:], maskv, op=ALU.mult),
                reads=[B_logit, B_lns[0]], writes=[B_gates])
            for tt in range(16):
                ph = (ps_a if tt < 8 else ps_b)[0:4, (tt % 8) * 128:
                                                (tt % 8 + 1) * 128]
                bh = (B_psa2 if tt < 8 else B_psb2)[(tt % 8) // 4]
                g.add("tensor", lambda e, tt=tt, ph=ph: e.transpose(
                    ph, gates[:, tt * 4:(tt + 1) * 4], ident32[:]),
                    reads=[B_gates, B_w["ident32"]], writes=[bh])
            gatesT = vs[0][0:4, :]
            g.add("vector", lambda e, gatesT=gatesT: e.tensor_copy(
                gatesT[:, 0:1024], ps_a[0:4, :]),
                reads=B_psa2, writes=B_vsb)
            g.add("vector", lambda e, gatesT=gatesT: e.tensor_copy(
                gatesT[:, 1024:2048], ps_b[0:4, :]),
                reads=B_psb2, writes=B_vsb)
            gaterep = qdup[0]
            for s in range(4):
                pbh = ps_b[:, (s % 2) * 512:(s % 2) * 512 + 512]
                bh = B_psb2[s % 2]
                g.add("tensor", lambda e, s=s, pbh=pbh, gatesT=gatesT: e.matmul(
                    pbh, selm[:], gatesT[:, s * 512:(s + 1) * 512],
                    start=True, stop=True),
                    reads=B_vsb + [B_w["selm"]], writes=[bh])
                g.add("vector", lambda e, s=s, pbh=pbh, gaterep=gaterep: e.tensor_copy(
                    gaterep[:, s * 512:(s + 1) * 512], pbh),
                    reads=[bh], writes=[B_qdup[0]])
            downb = ET[0]
            g.add("vector", lambda e, downb=downb, gaterep=gaterep: e.tensor_tensor(
                downb[:], ps_o[:], gaterep[:], op=ALU.mult),
                reads=[B_pso, B_qdup[0]], writes=[B_ET[0]])
            pend_stats = []
            for tc in range(4):
                for ot in range(8):
                    slot = (tc * 8 + ot) % 4
                    pb = (ps_a, ps_a, ps_b, ps_b)[slot][:, (slot % 2) * 512:
                                                        (slot % 2) * 512 + 512]
                    bps = (B_psa2[0], B_psa2[1], B_psb2[0], B_psb2[1])[slot]
                    for d in range(DT):
                        g.add("tensor", lambda e, pb=pb, ot=ot, tc=tc, d=d: e.matmul(
                            pb, wwt[:, d * 1024 + ot * 128:d * 1024 + (ot + 1) * 128],
                            hs(d, tc * 512, (tc + 1) * 512),
                            start=(d == 0), stop=False),
                            reads=[B_hT[d], B_w["wwt"]], writes=[bps])
                    g.add("tensor", lambda e, pb=pb, ot=ot, tc=tc, downb=downb: e.matmul(
                        pb, wup[:, ot * 128:(ot + 1) * 128],
                        downb[:, tc * 512:(tc + 1) * 512],
                        start=False, stop=True),
                        reads=[B_ET[0], B_w["wup"]], writes=[bps])
                    g.add("vector", lambda e, pb=pb, ot=ot, tc=tc: e.tensor_tensor(
                        xs(ot, tc * 512, (tc + 1) * 512),
                        xs(ot, tc * 512, (tc + 1) * 512), pb, op=ALU.add),
                        reads=[bps, B_xT[ot][tc]], writes=[B_xT[ot][tc]])
                yv = dr[f"y{ib}"].ap()
                g.add("dma_s", lambda e, yv=yv, tc=tc: e.dma_start(
                    yv[:, tc * 4096:(tc + 1) * 4096],
                    xT[:, tc * 4096:(tc + 1) * 4096]),
                    reads=[B_xT[d][tc] for d in range(DT)])
                if ib + 1 < NB:
                    xv = dr[f"x{ib + 1}"].ap()
                    g.add("dma_s", lambda e, xv=xv, tc=tc: e.dma_start(
                        xT[:, tc * 4096:(tc + 1) * 4096],
                        xv[:, tc * 4096:(tc + 1) * 4096]),
                        writes=[B_xT[d][tc] for d in range(DT)])
                    pend_stats.append(tc)
                while pend_stats and tc - pend_stats[0] >= 2:
                    ln_stats(pend_stats.pop(0))
            for tcp in pend_stats:
                ln_stats(tcp)

        for ib in range(NB):
            if ib == 0:
                xv = dr["x0"].ap()
                for tc in range(4):
                    g.add("dma_s", lambda e, xv=xv, tc=tc: e.dma_start(
                        xT[:, tc * 4096:(tc + 1) * 4096],
                        xv[:, tc * 4096:(tc + 1) * 4096]),
                        writes=[B_xT[d][tc] for d in range(DT)])
                for tc in range(4):
                    ln_stats(tc)
            ln_apply()
            for h in range(H):
                attn_head(ib, h)
                if ib == 0 and h == 0:
                    prep_emit()
                if h % 2 == 1:
                    ln_stats(h // 2)
            ln_apply()
            ffn_phase(ib)

        if DEBUG_TAPS:
            for tap in DEBUG_TAPS.split(","):
                if tap == "hT":
                    dbg = nc.dram_tensor("dbg_hT", [128, DT * C], F32,
                                         kind="ExternalOutput")
                    g.add("dma_s", lambda e, dbg=dbg: e.dma_start(
                        dbg.ap(), hT[:]), reads=B_hT)

        g.finalize()

        sems = {}
        for chan in ENGS:
            sems[chan] = ctx.enter_context(nc.semaphore(f"sem_{chan}"))
        dma_sems = [t.sem for t in g.tasks if t.chan in DMAQ]
        for sname in dma_sems:
            sems[sname] = ctx.enter_context(nc.semaphore(f"sem_{sname}"))

        def epilogue(e):
            for sname in dma_sems:
                e.wait_ge(sems[sname], 16)
            return None
        t = Task(len(g.tasks), "sync", epilogue)
        t.sem = "sync"
        g.tasks.append(t)

        g.emit(nc, sems)
    return nc


_NC_CACHE = None


def kernel(**inputs):
    global _NC_CACHE
    in_maps = host_prep(inputs)
    if _NC_CACHE is None:
        _NC_CACHE = build_program()
    nc = _NC_CACHE
    res = run_bass_kernel_spmd(nc, in_maps, core_ids=list(range(NCORES)))
    out = np.empty((NCORES * NB, C, D), dtype=np.float32)
    for c in range(NCORES):
        for ib in range(NB):
            yb = res.results[c][f"y{ib}"]
            out[NB * c + ib] = yb.reshape(128, 4, 8, 512).transpose(
                1, 3, 2, 0).reshape(C, D)
    return out


# version 43
# speedup vs baseline: 1.1838x; 1.0073x over previous
import os
import numpy as np
import concourse.bass as bass
import concourse.mybir as mybir
from concourse.bass_utils import run_bass_kernel_spmd

F32 = mybir.dt.float32
BF16 = mybir.dt.bfloat16
AF = mybir.ActivationFunctionType
ALU = mybir.AluOpType
AX = mybir.AxisListType.X

NCORES = 8
NB = 2
C = 2048
D = 1024
H = 8
DT = 8
LN_EPS = 1e-5
SCALE = 0.125
CLAMP_MAX = float(np.log(1.0 / 0.01))

DEBUG_TAPS = os.environ.get("KMK_DEBUG_TAPS", "")
PH = int(os.environ.get("KMK_PH", "99"))


class Buf:
    __slots__ = ("name", "w", "readers")

    def __init__(self, name):
        self.name = name
        self.w = None
        self.readers = []


class Task:
    __slots__ = ("idx", "chan", "fn", "deps", "inc", "count", "sem")

    def __init__(self, idx, chan, fn):
        self.idx = idx
        self.chan = chan
        self.fn = fn
        self.deps = {}
        self.inc = False
        self.count = None
        self.sem = None


ENGS = ("tensor", "vector", "scalar", "gpsimd", "sync")
DMAQ = {"dma_s": "sync", "dma_g": "gpsimd", "dma_a": "scalar"}


def issue_engine(chan):
    return DMAQ.get(chan, chan)


class Graph:
    def __init__(self):
        self.tasks = []
        self.n_dma = 0

    def add(self, chan, fn, reads=(), writes=(), after=()):
        t = Task(len(self.tasks), chan, fn)
        if chan in DMAQ:
            t.sem = f"dma{self.n_dma}"
            self.n_dma += 1
        else:
            t.sem = chan
        self.tasks.append(t)
        for b in reads:
            if b.w is not None:
                t.deps[b.w] = 2
            b.readers.append(t.idx)
        for b in writes:
            if b.w is not None:
                t.deps[b.w] = max(t.deps.get(b.w, 0), 0)
            for r in b.readers:
                if r != t.idx:
                    t.deps[r] = max(t.deps.get(r, 0), 1)
            b.w = t.idx
            b.readers = []
        for p in after:
            t.deps[p] = 2
        return t.idx

    def finalize(self):
        for t in self.tasks:
            if t.chan in DMAQ:
                t.inc = True
        for t in self.tasks:
            for p, sev in t.deps.items():
                pt = self.tasks[p]
                if pt.chan in DMAQ:
                    need = True
                else:
                    need = sev >= 1 or (pt.chan != issue_engine(t.chan))
                if need:
                    pt.inc = True
        cnt = {}
        for t in self.tasks:
            if t.inc:
                step = 16 if t.chan in DMAQ else 1
                cnt[t.sem] = cnt.get(t.sem, 0) + step
                t.count = cnt[t.sem]
        self.final_counts = cnt

    def emit(self, nc, sems):
        by_eng = {e: [] for e in ENGS}
        for t in self.tasks:
            by_eng[issue_engine(t.chan)].append(t)
        watermark = {}

        def emit_engine(eng_name, eng):
            for t in by_eng[eng_name]:
                req = {}
                for p, sev in t.deps.items():
                    pt = self.tasks[p]
                    if pt.chan in DMAQ:
                        need = True
                    else:
                        need = sev >= 1 or (pt.chan != eng_name)
                    if not need:
                        continue
                    assert pt.inc and pt.count is not None
                    req[pt.sem] = max(req.get(pt.sem, 0), pt.count)
                for chan in sorted(req):
                    v = req[chan]
                    key = (eng_name, chan)
                    if watermark.get(key, 0) >= v:
                        continue
                    eng.wait_ge(sems[chan], v)
                    watermark[key] = v
                inst = t.fn(eng)
                if t.inc:
                    assert inst is not None, f"task {t.idx} on {t.chan} has no inst"
                    step = 16 if t.chan in DMAQ else 1
                    inst.then_inc(sems[t.sem], step)

        with nc.Block() as blk:
            blk.tensor(lambda e: emit_engine("tensor", e))
            blk.vector(lambda e: emit_engine("vector", e))
            blk.scalar(lambda e: emit_engine("scalar", e))
            blk.gpsimd(lambda e: emit_engine("gpsimd", e))
            blk.sync(lambda e: emit_engine("sync", e))


def _wtile(a, free):
    return np.ascontiguousarray(
        a.reshape(DT, 128, free).transpose(1, 0, 2).reshape(128, DT * free))


def host_prep(inp):
    x = inp["L0_attn_x"]
    w = {}
    mq = inp["L0_attn_Mq"]
    mqd = np.ascontiguousarray(np.broadcast_to(
        mq.reshape(D, 8, 1, 64), (D, 8, 2, 64)).reshape(D, 1024))
    w["wqd"] = _wtile(mqd, 1024)
    w["wk"] = _wtile(inp["L0_attn_Mk"], 512)
    w["wv"] = _wtile(inp["L0_attn_Mv"], 1024)
    w["wwt"] = _wtile(np.ascontiguousarray(inp["L0_ffn_W"].T), 1024)
    w["wp"] = _wtile(np.ascontiguousarray(inp["L0_ffn_Wp"].T), 256)
    ranks = (1, 2, 4, 8)
    wdpad = np.zeros((128, D), dtype=np.float32)
    wupad = np.zeros((D, 128), dtype=np.float32)
    for e, r in enumerate(ranks):
        wdpad[32 * e:32 * e + r, :] = inp[f"L0_ffn_Wd{e}"]
        wupad[:, 32 * e:32 * e + r] = inp[f"L0_ffn_Wu{e}"]
    w["wdp"] = _wtile(np.ascontiguousarray(wdpad.T), 128)
    w["wup"] = np.ascontiguousarray(wupad.T)
    w["simw"] = np.ascontiguousarray(
        inp["L0_ffn_sim"].reshape(2, 128, 4).transpose(1, 0, 2).reshape(128, 8))
    w["bpw"] = np.ascontiguousarray(inp["L0_ffn_bp"].reshape(2, 128).T)
    w["tempw"] = np.ascontiguousarray(inp["L0_ffn_temp"].reshape(1, 1))
    w["onesm"] = np.full((128, 128), 1.0 / 1024.0, dtype=np.float32)

    w["selm"] = np.ascontiguousarray(
        (np.arange(4)[:, None] == (np.arange(128)[None, :] // 32)).astype(np.float32))
    w["identc"] = np.eye(128, dtype=np.float32)
    w["onesrow"] = np.ones((1, 128), dtype=np.float32)

    in_maps = []
    for c in range(NCORES):
        m = dict(w)
        for ib in range(NB):
            xb = x[NB * c + ib]
            m[f"x{ib}"] = np.ascontiguousarray(
                xb.reshape(4, 512, 8, 128).transpose(3, 0, 2, 1).reshape(128, 16384))
        in_maps.append(m)
    return in_maps


def build_program():
    nc = bass.Bass("TRN2", target_bir_lowering=False)

    dr = {}
    for ib in range(NB):
        dr[f"x{ib}"] = nc.dram_tensor(f"x{ib}", [128, DT * C], F32,
                                      kind="ExternalInput")
        dr[f"y{ib}"] = nc.dram_tensor(f"y{ib}", [128, DT * C], F32,
                                      kind="ExternalOutput")
    for name, shape in [
        ("wqd", [128, DT * 1024]), ("wk", [128, DT * 512]), ("wv", [128, DT * 1024]),
        ("wwt", [128, DT * 1024]), ("wp", [128, DT * 256]), ("wdp", [128, DT * 128]),
        ("wup", [128, 1024]), ("simw", [128, 8]), ("bpw", [128, 2]),
        ("tempw", [1, 1]), ("onesm", [128, 128]), ("selm", [4, 128]),
        ("identc", [128, 128]), ("onesrow", [1, 128]),
    ]:
        dr[name] = nc.dram_tensor(name, shape, F32, kind="ExternalInput")

    taps = []
    from contextlib import ExitStack
    ctx = ExitStack()
    sb = {}

    def S(name, shape, dtype):
        sb[name] = ctx.enter_context(nc.sbuf_tensor("s_" + name, shape, dtype))
        return sb[name]

    with ctx:
        xT = S("xT", [128, DT * C], F32)
        hT = S("hT", [128, DT * C], BF16)
        wqd = S("wqd", [128, DT * 1024], BF16)
        wk = S("wk", [128, DT * 512], BF16)
        wv = S("wv", [128, DT * 1024], BF16)
        wwt = S("wwt", [128, DT * 1024], BF16)
        wp = S("wp", [128, DT * 256], BF16)
        wdp = S("wdp", [128, DT * 128], BF16)
        wup = S("wup", [128, 1024], BF16)
        simb = S("simb", [128, 8], BF16)
        bpb = S("bpb", [128, 2], F32)
        tempb = S("tempb", [1, 1], F32)
        onesm = S("onesm", [128, 128], BF16)
        onesm32 = S("onesm32", [128, 128], F32)
        selm = S("selm", [4, 128], BF16)
        identb = S("identb", [128, 128], BF16)
        ident32 = S("ident32", [128, 128], F32)
        onesrow = S("onesrow", [1, 128], BF16)
        xsq = [S(f"xsq{i}", [128, 512], BF16) for i in range(2)]
        lns = [S("lns0", [128, 512], F32)]
        epsb = S("epsb", [128, 1], F32)
        tinyb = S("tinyb", [128, 1], F32)
        kzp = [[S("kzE0", [128, 1024], BF16), S("kzO0", [128, 1024], BF16)]]
        m_b = S("m_b", [128, C], BF16)
        rinvb = S("rinvb", [128, C], BF16)
        qdup = [S("qdup0", [128, 2048], BF16)]
        ET = [S(f"ET{i}", [128, 2048], BF16) for i in range(4)]
        spart = S("spart", [128, 32], F32)
        ssum = S("ssum", [128, 16], F32)
        srec = S("srec", [128, 16], F32)
        projT = S("projT", [128, 2 * C], BF16)
        vh = [projT[:, 0:2048]]
        vs = [projT[:, 2048:4096]]
        sqp = [S("sqp0", [128, 512], BF16)]
        logit = S("logit", [128, 64], F32)
        sq8 = S("sq8", [128, 8], BF16)
        lmax = lns[0][:, 192:208]
        lnsim = lns[0][:, 208:212]
        tmin = lns[0][0:1, 212:213]
        texp = lns[0][0:1, 213:214]
        screp = lns[0][:, 214:215]
        rsimb = S("rsimb", [128, 4], BF16)
        texb = S("texb", [1, 1], BF16)
        simns = S("simns", [128, 8], BF16)

        ps_a = ctx.enter_context(nc.psum_tensor("ps_a", [128, 1024], F32))
        ps_b = ctx.enter_context(nc.psum_tensor("ps_b", [128, 1024], F32))
        ps_o = ctx.enter_context(nc.psum_tensor("ps_o", [128, 2048], F32))

        g = Graph()

        B_xT = [[Buf(f"xT{d}_{t}") for t in range(4)] for d in range(DT)]
        B_hT = [Buf(f"hT{d}") for d in range(DT)]
        B_w = {n: Buf(n) for n in ["wqd", "wk", "wv", "wwt", "wp", "wdp", "wup",
                                   "simb", "bpb", "tempb", "onesm", "onesm32",
                                   "selm", "identb", "onesrow", "epsb",
                                   "tinyb", "ident32"]}
        B_xsq = [Buf(f"xsq{i}") for i in range(2)]
        B_mb = [Buf("m_b_et1")] * 4
        B_rinv = [Buf("rinv_et2")] * 4
        B_lns = [Buf("lns0")]
        B_psa2 = [Buf("ps_a0"), Buf("ps_a1")]
        B_psb2 = [Buf("ps_b0"), Buf("ps_b1")]
        B_psa = B_psa2
        B_psb = B_psb2
        B_pso = Buf("ps_o")
        B_kz4 = [[Buf(f"kz{par}_{ot}") for ot in range(4)] for par in range(2)]
        B_qdup2 = [Buf("qdup_h0"), Buf("qdup_h1")]
        B_qdup = [B_qdup2[0]]
        B_vsb = [Buf(f"vs_{b}") for b in range(8)]
        B_ET = [Buf(f"ET{i}") for i in range(4)]

        B_sp = [Buf(f"spart{j}") for j in range(16)]
        B_ssum = [Buf(f"ssum{b}") for b in range(8)]
        B_srec = [Buf(f"srec{b}") for b in range(8)]
        B_projT = Buf("projT")
        B_vh = [B_projT]
        B_sqp = [Buf("sqp0")]
        B_logit = Buf("logit")
        B_ev = B_lns[0]
        B_lmax = Buf("lmax")
        B_gates = B_lns[0]
        B_prep = Buf("simprep")

        def dma_cast(name_d, dst, bufs):
            g.add("dma_g", lambda e, s=dst, dn=name_d: e.dma_start(s[:], dr[dn].ap()),
                  writes=bufs)

        dma_cast("onesm", onesm, [B_w["onesm"]])
        dma_cast("wqd", wqd, [B_w["wqd"]])
        dma_cast("wk", wk, [B_w["wk"]])
        dma_cast("wv", wv, [B_w["wv"]])
        dma_cast("wwt", wwt, [B_w["wwt"]])
        dma_cast("wp", wp, [B_w["wp"]])
        dma_cast("wdp", wdp, [B_w["wdp"]])
        dma_cast("wup", wup, [B_w["wup"]])
        dma_cast("simw", simb, [B_w["simb"]])
        dma_cast("selm", selm, [B_w["selm"]])
        dma_cast("identc", identb, [B_w["identb"]])
        dma_cast("onesrow", onesrow, [B_w["onesrow"]])
        g.add("dma_s", lambda e: e.dma_start(bpb[:], dr["bpw"].ap()), writes=[B_w["bpb"]])
        g.add("dma_s", lambda e: e.dma_start(onesm32[:], dr["onesm"].ap()),
              writes=[B_w["onesm32"]])
        g.add("dma_s", lambda e: e.dma_start(ident32[:], dr["identc"].ap()),
              writes=[B_w["ident32"]])
        g.add("dma_s", lambda e: e.dma_start(tempb[:], dr["tempw"].ap()),
              writes=[B_w["tempb"]])
        g.add("vector", lambda e: e.memset(epsb[:], LN_EPS), writes=[B_w["epsb"]])
        g.add("vector", lambda e: e.memset(kzp[0][0][64:128, :], 0.0),
              writes=B_kz4[0])
        g.add("vector", lambda e: e.memset(kzp[0][1][0:64, :], 0.0),
              writes=B_kz4[1])
        g.add("vector", lambda e: e.memset(tinyb[:], 1e-30), writes=[B_w["tinyb"]])

        def prep_emit():
            g.add("vector", lambda e: e.tensor_tensor(sq8[:], simb[:], simb[:], op=ALU.mult),
                  reads=[B_w["simb"]], writes=[B_prep])
            g.add("tensor", lambda e: e.matmul(ps_a[:, 0:4], onesm[:], sq8[:, 0:4],
                                               start=True, stop=False),
                  reads=[B_prep, B_w["onesm"]], writes=[B_psa2[0]])
            g.add("tensor", lambda e: e.matmul(ps_a[:, 0:4], onesm[:], sq8[:, 4:8],
                                               start=False, stop=True),
                  reads=[B_prep, B_w["onesm"]], writes=[B_psa2[0]])
            g.add("scalar", lambda e: e.activation(lnsim[:], ps_a[:, 0:4], AF.Ln,
                                                   bias=tinyb[:], scale=1024.0),
                  reads=[B_psa2[0], B_w["tinyb"]], writes=[B_prep])
            g.add("scalar", lambda e: e.activation(rsimb[:], lnsim[:], AF.Exp,
                                                   bias=0.0, scale=-0.5),
                  reads=[B_prep], writes=[B_prep])
            g.add("vector", lambda e: e.tensor_scalar(tmin[:], tempb[:], CLAMP_MAX, None,
                                                      op0=ALU.min),
                  reads=[B_w["tempb"]], writes=[B_prep])
            g.add("scalar", lambda e: e.activation(texp[:], tmin[:], AF.Exp),
                  reads=[B_prep], writes=[B_prep])
            g.add("vector", lambda e: e.tensor_copy(texb[:], texp[:]),
                  reads=[B_prep], writes=[B_prep])
            g.add("tensor", lambda e: e.matmul(ps_b[:, 0:1], onesrow[:], texb[:],
                                               start=True, stop=True),
                  reads=[B_prep, B_w["onesrow"]], writes=[B_psb2[0]])
            g.add("vector", lambda e: e.tensor_copy(screp[:], ps_b[:, 0:1]),
                  reads=[B_psb2[0]], writes=[B_prep])
            for pt in range(2):
                g.add("vector", lambda e, pt=pt: e.tensor_tensor(
                    simns[:, 4 * pt:4 * pt + 4], simb[:, 4 * pt:4 * pt + 4], rsimb[:],
                    op=ALU.mult),
                    reads=[B_w["simb"], B_prep], writes=[B_prep])
                g.add("vector", lambda e, pt=pt: e.tensor_scalar(
                    simns[:, 4 * pt:4 * pt + 4], simns[:, 4 * pt:4 * pt + 4], screp[:],
                    None, op0=ALU.mult),
                    reads=[B_prep], writes=[B_prep])

        def xs(d, c0, c1):
            tc = c0 // 512
            assert (c1 - 1) // 512 == tc
            base = tc * 4096 + d * 512 + (c0 % 512)
            return xT[:, base:base + (c1 - c0)]

        def hs(d, c0, c1):
            return hT[:, d * C + c0:d * C + c1]

        def ln_stats(tc, sq_act=False):
            c0 = tc * 512
            mp = (ps_a if tc % 2 == 0 else ps_b)[:, 0:512]
            sp = (ps_a if tc % 2 == 0 else ps_b)[:, 512:1024]
            bpsl = B_psa2 if tc % 2 == 0 else B_psb2
            for d in range(DT):
                i = d % 2
                if sq_act:
                    g.add("scalar", lambda e, d=d, c0=c0, i=i: e.activation(
                        xsq[i][:], xs(d, c0, c0 + 512), AF.Square),
                        reads=[B_xT[d][tc]], writes=[B_xsq[i]])
                else:
                    g.add("gpsimd", lambda e, d=d, c0=c0, i=i: e.tensor_tensor(
                        xsq[i][:], xs(d, c0, c0 + 512), xs(d, c0, c0 + 512),
                        op=ALU.mult),
                        reads=[B_xT[d][tc]], writes=[B_xsq[i]])
                g.add("tensor", lambda e, mp=mp, d=d, c0=c0: e.matmul(
                    mp, onesm32[:], xs(d, c0, c0 + 512),
                    start=(d == 0), stop=(d == DT - 1)),
                    reads=[B_xT[d][tc], B_w["onesm32"]], writes=[bpsl[0]])
                g.add("tensor", lambda e, sp=sp, i=i, d=d: e.matmul(
                    sp, onesm[:], xsq[i][:], start=(d == 0), stop=(d == DT - 1)),
                    reads=[B_xsq[i], B_w["onesm"]], writes=[bpsl[1]])
            li = 0
            g.add("vector", lambda e, mp=mp, c0=c0: e.tensor_copy(
                m_b[:, c0:c0 + 512], mp),
                reads=[bpsl[0]], writes=[B_mb[tc]])
            g.add("vector", lambda e, c0=c0, li=li: e.tensor_tensor(
                lns[li][:], m_b[:, c0:c0 + 512], m_b[:, c0:c0 + 512], op=ALU.mult),
                reads=[B_mb[tc]], writes=[B_lns[li]])
            g.add("vector", lambda e, sp=sp, li=li: e.tensor_tensor(
                lns[li][:], sp, lns[li][:], op=ALU.subtract),
                reads=[bpsl[1], B_lns[li]], writes=[B_lns[li]])
            g.add("scalar", lambda e, li=li: e.activation(
                lns[li][:], lns[li][:], AF.Ln, bias=epsb[:], scale=1.0),
                reads=[B_lns[li], B_w["epsb"]], writes=[B_lns[li]])
            g.add("scalar", lambda e, c0=c0, li=li: e.activation(
                rinvb[:, c0:c0 + 512], lns[li][:], AF.Exp, bias=0.0, scale=-0.5),
                reads=[B_lns[li]], writes=[B_rinv[tc]])
        def ln_apply():
            for d in range(DT):
                hv = bass.AP(hT, d * C, [[DT * C, 128], [512, 4], [1, 512]])
                xv = bass.AP(xT, d * 512, [[DT * C, 128], [4096, 4], [1, 512]])
                mv = bass.AP(m_b, 0, [[C, 128], [512, 4], [1, 512]])
                rv = bass.AP(rinvb, 0, [[C, 128], [512, 4], [1, 512]])
                g.add("vector", lambda e, hv=hv, xv=xv, mv=mv: e.tensor_tensor(
                    hv, xv, mv, op=ALU.subtract),
                    reads=B_xT[d] + B_mb, writes=[B_hT[d]])
                g.add("vector", lambda e, d=d: e.tensor_tensor(
                    hs(d, 0, C), hs(d, 0, C), rinvb[:, :], op=ALU.mult),
                    reads=[B_hT[d]] + B_rinv, writes=[B_hT[d]])

        def ln_phase(ib):
            for tc in range(4):
                ln_stats(tc)
            ln_apply()

        def attn_head(ib, h):
            hb = 0
            c0 = h * 256
            vunits = []

            def qd_half(half, douter):
                if douter:
                    for d in range(DT):
                        for b4 in range(4):
                            b = half * 4 + b4
                            g.add("tensor", lambda e, b=b, b4=b4, d=d: e.matmul(
                                ps_a[:, b4 * 256:(b4 + 1) * 256],
                                wqd[:, d * 1024 + b * 128:d * 1024 + (b + 1) * 128],
                                hs(d, c0, c0 + 256),
                                start=(d == 0 and b4 % 2 == 0),
                                stop=(d == DT - 1 and b4 % 2 == 1)),
                                reads=[B_hT[d], B_w["wqd"]], writes=[B_psa2[b4 // 2]])
                else:
                    for b4 in range(4):
                        b = half * 4 + b4
                        for d in range(DT):
                            g.add("tensor", lambda e, b=b, b4=b4, d=d: e.matmul(
                                ps_a[:, b4 * 256:(b4 + 1) * 256],
                                wqd[:, d * 1024 + b * 128:d * 1024 + (b + 1) * 128],
                                hs(d, c0, c0 + 256),
                                start=(d == 0), stop=(d == DT - 1)),
                                reads=[B_hT[d], B_w["wqd"]], writes=[B_psa2[b4 // 2]])
                g.add("vector", lambda e, half=half: e.tensor_copy(
                    qdup[0][:, half * 1024:(half + 1) * 1024], ps_a[:]),
                    reads=B_psa2, writes=[B_qdup2[half]])

            def k_all(douter):
                if douter:
                    for d in range(DT):
                        for ot in range(4):
                            g.add("tensor", lambda e, ot=ot, d=d: e.matmul(
                                ps_b[:, ot * 256:(ot + 1) * 256],
                                wk[:, d * 512 + ot * 128:d * 512 + (ot + 1) * 128],
                                hs(d, c0, c0 + 256),
                                start=(d == 0 and ot % 2 == 0),
                                stop=(d == DT - 1 and ot % 2 == 1)),
                                reads=[B_hT[d], B_w["wk"]], writes=[B_psb2[ot // 2]])
                else:
                    for ot in range(4):
                        for d in range(DT):
                            g.add("tensor", lambda e, ot=ot, d=d: e.matmul(
                                ps_b[:, ot * 256:(ot + 1) * 256],
                                wk[:, d * 512 + ot * 128:d * 512 + (ot + 1) * 128],
                                hs(d, c0, c0 + 256),
                                start=(d == 0), stop=(d == DT - 1)),
                                reads=[B_hT[d], B_w["wk"]], writes=[B_psb2[ot // 2]])
                g.add("vector", lambda e: e.tensor_copy(
                    kzp[0][0][0:64, :], ps_b[0:64, :]),
                    reads=B_psb2, writes=list(B_kz4[0]))
                g.add("vector", lambda e: e.tensor_copy(
                    kzp[0][1][64:128, :], ps_b[64:128, :]),
                    reads=B_psb2, writes=list(B_kz4[1]))

            first = (h == 0)
            if first:
                for d in range(DT):
                    for b4 in range(4):
                        g.add("tensor", lambda e, b4=b4, d=d: e.matmul(
                            ps_a[:, b4 * 256:(b4 + 1) * 256],
                            wqd[:, d * 1024 + b4 * 128:d * 1024 + (b4 + 1) * 128],
                            hs(d, c0, c0 + 256),
                            start=(d == 0 and b4 % 2 == 0),
                            stop=(d == DT - 1 and b4 % 2 == 1)),
                            reads=[B_hT[d], B_w["wqd"]], writes=[B_psa2[b4 // 2]])
                    for ot in range(4):
                        g.add("tensor", lambda e, ot=ot, d=d: e.matmul(
                            ps_b[:, ot * 256:(ot + 1) * 256],
                            wk[:, d * 512 + ot * 128:d * 512 + (ot + 1) * 128],
                            hs(d, c0, c0 + 256),
                            start=(d == 0 and ot % 2 == 0),
                            stop=(d == DT - 1 and ot % 2 == 1)),
                            reads=[B_hT[d], B_w["wk"]], writes=[B_psb2[ot // 2]])
                    for s4 in range(4):
                        tt, s = s4 // 2, s4 % 2
                        g.add("tensor", lambda e, tt=tt, d=d, s=s: e.matmul(
                            ps_o[:, tt * 1024 + s * 512:tt * 1024 + (s + 1) * 512],
                            hs(d, c0 + tt * 128, c0 + (tt + 1) * 128),
                            wv[:, d * 1024 + s * 512:d * 1024 + (s + 1) * 512],
                            start=(d == 0), stop=(d == DT - 1)),
                            reads=[B_hT[d], B_w["wv"]], writes=[B_pso])
                g.add("vector", lambda e: e.tensor_copy(
                    qdup[0][:, 0:1024], ps_a[:]),
                    reads=B_psa2, writes=[B_qdup2[0]])
                g.add("vector", lambda e: e.tensor_copy(
                    kzp[0][0][0:64, :], ps_b[0:64, :]),
                    reads=B_psb2, writes=list(B_kz4[0]))
                g.add("vector", lambda e: e.tensor_copy(
                    kzp[0][1][64:128, :], ps_b[64:128, :]),
                    reads=B_psb2, writes=list(B_kz4[1]))
                g.add("vector", lambda e: e.tensor_copy(vh[hb][:], ps_o[:]),
                      reads=[B_pso], writes=[B_projT])
                qd_half(1, False)
            else:
                qd_half(0, False)
                k_all(False)
                qd_half(1, False)
                for tt in range(2):
                    for s in range(2):
                        unit = []
                        for d in range(DT):
                            unit.append(lambda tt=tt, d=d, s=s: g.add(
                                "tensor", lambda e, tt=tt, d=d, s=s: e.matmul(
                                    ps_o[:, tt * 1024 + s * 512:
                                         tt * 1024 + (s + 1) * 512],
                                    hs(d, c0 + tt * 128, c0 + (tt + 1) * 128),
                                    wv[:, d * 1024 + s * 512:
                                       d * 1024 + (s + 1) * 512],
                                    start=(d == 0), stop=(d == DT - 1)),
                                reads=[B_hT[d], B_w["wv"]], writes=[B_pso]))
                        if s == 1:
                            unit.append(lambda tt=tt: g.add(
                                "vector", lambda e, tt=tt: e.tensor_copy(
                                    vh[hb][:, tt * 1024:(tt + 1) * 1024],
                                    ps_o[:, tt * 1024:(tt + 1) * 1024]),
                                reads=[B_pso], writes=[B_projT]))
                        vunits.append(unit)
            pend = []

            def push_out(bj):
                for j2 in (2 * bj, 2 * bj + 1):
                    vcol = (j2 % 2) * 1024 + (j2 // 2) * 128
                    for s in range(4):
                        pend.append((j2, lambda j2=j2, vcol=vcol, s=s: g.add(
                            "tensor", lambda e, j2=j2, vcol=vcol, s=s: e.matmul(
                                ps_o[:, s * 512:(s + 1) * 512],
                                vs[0][:, vcol:vcol + 128],
                                ET[j2 % 4][:, s * 512:(s + 1) * 512],
                                start=(j2 == 0), stop=(j2 == 15)),
                            reads=[B_vsb[j2 // 2], B_ET[j2 % 4]], writes=[B_pso])))

            def drain_out(n):
                for _ in range(min(n, len(pend))):
                    pend.pop(0)[1]()

            for jt in range(16):
                while pend and pend[0][0] <= jt - 4:
                    pend.pop(0)[1]()
                pj = (jt // 2) % 2
                lhs_c = (jt // 4) * 256 + (jt % 2) * 128
                for ihf in range(2):
                    pbuf = (ps_a, ps_b)[(2 * jt + ihf) % 2]
                    bl = (B_psa2, B_psb2)[(2 * jt + ihf) % 2]
                    for half in range(2):
                        g.add("tensor", lambda e, pbuf=pbuf, pj=pj, lhs_c=lhs_c,
                              half=half, ihf=ihf: e.matmul(
                                  pbuf[:, half * 512:(half + 1) * 512],
                                  kzp[0][pj][:, lhs_c:lhs_c + 128],
                                  qdup[0][:, ihf * 1024 + half * 512:
                                       ihf * 1024 + (half + 1) * 512],
                                  start=True, stop=True),
                              reads=[B_kz4[pj][jt // 4], B_qdup2[ihf]],
                              writes=[bl[half]])
                    g.add("scalar", lambda e, pbuf=pbuf, jt=jt, ihf=ihf: e.activation(
                        ET[jt % 4][:, ihf * 1024:(ihf + 1) * 1024], pbuf[:], AF.Exp,
                        bias=0.0, scale=SCALE,
                        accum_out=spart[:, 2 * jt + ihf:2 * jt + ihf + 1]),
                        reads=bl, writes=[B_ET[jt % 4], B_sp[jt]])
                    if vunits:
                        for th in vunits.pop(0):
                            th()
                    else:
                        ndr = 0
                        while pend and len(pend) > 16 and ndr < 2:
                            pend.pop(0)[1]()
                            ndr += 1
                if jt % 2 == 1:
                    bj = jt // 2
                    g.add("vector", lambda e, bj=bj: e.tensor_tensor(
                        ssum[:, 2 * bj:2 * bj + 2],
                        spart[:, 4 * bj:4 * bj + 4:2],
                        spart[:, 4 * bj + 1:4 * bj + 4:2], op=ALU.add),
                        reads=[B_sp[jt - 1], B_sp[jt]], writes=[B_ssum[bj]])
                    g.add("vector", lambda e, bj=bj: e.reciprocal(
                        srec[:, 2 * bj:2 * bj + 2], ssum[:, 2 * bj:2 * bj + 2]),
                        reads=[B_ssum[bj]], writes=[B_srec[bj]])
                    g.add("vector", lambda e, bj=bj: e.tensor_tensor(
                        bass.AP(sb["projT"], 2048 + bj * 128,
                                [[2 * C, 128], [1024, 2], [1, 128]]),
                        bass.AP(sb["projT"], bj * 128,
                                [[2 * C, 128], [1024, 2], [1, 128]]),
                        srec[:, 2 * bj:2 * bj + 2].unsqueeze(2)
                        .broadcast_to([128, 2, 128]), op=ALU.mult),
                        reads=[B_projT, B_srec[bj]], writes=[B_vsb[bj]])
                    push_out(bj)
            drain_out(len(pend))
            xview = bass.AP(xT, (h // 2) * 4096 + (h % 2) * 256,
                            [[DT * C, 128], [512, 8], [1, 256]])
            oview = bass.AP(ps_o, 0, [[2048, 128], [256, 8], [1, 256]])
            xbufs = [B_xT[b][h // 2] for b in range(8)]
            g.add("vector", lambda e, xview=xview, oview=oview: e.tensor_tensor(
                xview, xview, oview, op=ALU.add),
                reads=[B_pso] + xbufs, writes=xbufs)

        def ffn_phase(ib):
            def p1ps(s):
                return (ps_a, ps_a, ps_b, ps_b)[s][:, (s % 2) * 512:
                                                   (s % 2) * 512 + 512]

            def p1buf(s):
                return (B_psa2[0], B_psa2[1], B_psb2[0], B_psb2[1])[s]
            for d in range(DT):
                for s in range(4):
                    g.add("tensor", lambda e, d=d, s=s: e.matmul(
                        ps_o[:, s * 512:(s + 1) * 512],
                        wp[:, d * 256:d * 256 + 128],
                        hs(d, s * 512, (s + 1) * 512),
                        start=(d == 0), stop=(d == DT - 1)),
                        reads=[B_hT[d], B_w["wp"]], writes=[B_pso])
                for s in range(4):
                    g.add("tensor", lambda e, d=d, s=s: e.matmul(
                        p1ps(s),
                        wp[:, d * 256 + 128:d * 256 + 256],
                        hs(d, s * 512, (s + 1) * 512),
                        start=(d == 0), stop=(d == DT - 1)),
                        reads=[B_hT[d], B_w["wp"]], writes=[p1buf(s)])
            g.add("vector", lambda e: e.tensor_scalar(
                projT[:, 0:C], ps_o[:], bpb[:, 0:1], None, op0=ALU.add),
                reads=[B_pso, B_w["bpb"]], writes=[B_projT] + B_vsb)
            g.add("vector", lambda e: e.tensor_scalar(
                projT[:, C:C + 1024], ps_a[:], bpb[:, 1:2], None, op0=ALU.add),
                reads=B_psa2 + [B_w["bpb"]], writes=[B_projT])
            g.add("vector", lambda e: e.tensor_scalar(
                projT[:, C + 1024:2 * C], ps_b[:], bpb[:, 1:2], None, op0=ALU.add),
                reads=B_psb2 + [B_w["bpb"]], writes=[B_projT])
            for d in range(DT // 2):
                for s in range(4):
                    g.add("tensor", lambda e, d=d, s=s: e.matmul(
                        ps_o[:, s * 512:(s + 1) * 512], wdp[:, d * 128:(d + 1) * 128],
                        hs(d, s * 512, (s + 1) * 512),
                        start=(d == 0), stop=False),
                        reads=[B_hT[d], B_w["wdp"]], writes=[B_pso])
            for tc in range(4):
                c0 = tc * 512
                nps = (ps_a if tc % 2 == 0 else ps_b)[:, 0:512]
                bps = B_psa2[0] if tc % 2 == 0 else B_psb2[0]
                for pt in range(2):
                    i = 0
                    g.add("gpsimd", lambda e, pt=pt, c0=c0, i=i: e.tensor_tensor(
                        sqp[i][:], projT[:, pt * C + c0:pt * C + c0 + 512],
                        projT[:, pt * C + c0:pt * C + c0 + 512], op=ALU.mult),
                        reads=[B_projT], writes=[B_sqp[i]])
                    g.add("tensor", lambda e, nps=nps, i=i, pt=pt: e.matmul(
                        nps, onesm[:], sqp[i][:], start=(pt == 0), stop=(pt == 1)),
                        reads=[B_sqp[i], B_w["onesm"]], writes=[bps])
                li = 0
                g.add("scalar", lambda e, nps=nps, li=li: e.activation(
                    lns[li][:], nps, AF.Ln, bias=tinyb[:], scale=1024.0),
                    reads=[bps, B_w["tinyb"]], writes=[B_lns[li]])
                g.add("scalar", lambda e, c0=c0, li=li: e.activation(
                    rinvb[:, c0:c0 + 512], lns[li][:], AF.Exp, bias=0.0, scale=-0.5),
                    reads=[B_lns[li]], writes=[B_rinv[tc]])
            for d in range(DT // 2, DT):
                for s in range(4):
                    g.add("tensor", lambda e, d=d, s=s: e.matmul(
                        ps_o[:, s * 512:(s + 1) * 512], wdp[:, d * 128:(d + 1) * 128],
                        hs(d, s * 512, (s + 1) * 512),
                        start=False, stop=(d == DT - 1)),
                        reads=[B_hT[d], B_w["wdp"]], writes=[B_pso])
            for pt in range(2):
                g.add("vector", lambda e, pt=pt: e.tensor_tensor(
                    projT[:, pt * C:(pt + 1) * C], projT[:, pt * C:(pt + 1) * C],
                    rinvb[:, :], op=ALU.mult),
                    reads=[B_projT] + B_rinv, writes=[B_projT])
            for tt in range(16):
                slot = tt % 4
                pps = (ps_a, ps_a, ps_b, ps_b)[slot][:, (slot % 2) * 512:
                                                      (slot % 2) * 512 + 4]
                bps = (B_psa2[0], B_psa2[1], B_psb2[0], B_psb2[1])[slot]
                for pt in range(2):
                    g.add("tensor", lambda e, pps=pps, pt=pt, tt=tt: e.matmul(
                        pps, projT[:, pt * C + tt * 128:pt * C + (tt + 1) * 128],
                        simns[:, pt * 4:pt * 4 + 4], start=(pt == 0), stop=(pt == 1)),
                        reads=[B_projT, B_prep], writes=[bps])
                g.add("vector", lambda e, pps=pps, tt=tt: e.tensor_copy(
                    logit[:, tt * 4:(tt + 1) * 4], pps),
                    reads=[bps], writes=[B_logit])
            l3 = logit.rearrange("p (a b) -> p a b", b=4)
            maskv = lns[0][:, 0:64]
            evals = lns[0][:, 64:128]
            gates = lns[0][:, 128:192]
            g.add("vector", lambda e: e.tensor_reduce(lmax[:], l3, axis=AX, op=ALU.max),
                  reads=[B_logit], writes=[B_lmax])
            g.add("scalar", lambda e: e.activation(evals[:], logit[:], AF.Exp),
                  reads=[B_logit], writes=[B_ev])
            g.add("vector", lambda e, maskv=maskv: e.tensor_tensor(
                maskv.rearrange("p (a b) -> p a b", b=4), l3,
                lmax[:, :].unsqueeze(2).broadcast_to([128, 16, 4]), op=ALU.is_equal),
                reads=[B_logit, B_lmax], writes=[B_lns[0]])
            g.add("vector", lambda e: e.tensor_reduce(
                srec[:, :], evals.rearrange("p (a b) -> p a b", b=4), axis=AX,
                op=ALU.add),
                reads=[B_ev], writes=B_srec)
            g.add("vector", lambda e: e.reciprocal(ssum[:, :], srec[:, :]),
                  reads=B_srec, writes=B_ssum)
            g.add("vector", lambda e: e.tensor_tensor(
                l3, evals.rearrange("p (a b) -> p a b", b=4),
                ssum[:, :].unsqueeze(2).broadcast_to([128, 16, 4]), op=ALU.mult),
                reads=[B_ev] + B_ssum, writes=[B_logit])
            g.add("vector", lambda e, maskv=maskv: e.tensor_tensor(
                gates[:], logit[:], maskv, op=ALU.mult),
                reads=[B_logit, B_lns[0]], writes=[B_gates])
            for tt in range(16):
                ph = (ps_a if tt < 8 else ps_b)[0:4, (tt % 8) * 128:
                                                (tt % 8 + 1) * 128]
                bh = (B_psa2 if tt < 8 else B_psb2)[(tt % 8) // 4]
                g.add("tensor", lambda e, tt=tt, ph=ph: e.transpose(
                    ph, gates[:, tt * 4:(tt + 1) * 4], ident32[:]),
                    reads=[B_gates, B_w["ident32"]], writes=[bh])
            gatesT = vs[0][0:4, :]
            g.add("vector", lambda e, gatesT=gatesT: e.tensor_copy(
                gatesT[:, 0:1024], ps_a[0:4, :]),
                reads=B_psa2, writes=B_vsb)
            g.add("vector", lambda e, gatesT=gatesT: e.tensor_copy(
                gatesT[:, 1024:2048], ps_b[0:4, :]),
                reads=B_psb2, writes=B_vsb)
            gaterep = qdup[0]
            for s in range(4):
                pbh = ps_b[:, (s % 2) * 512:(s % 2) * 512 + 512]
                bh = B_psb2[s % 2]
                g.add("tensor", lambda e, s=s, pbh=pbh, gatesT=gatesT: e.matmul(
                    pbh, selm[:], gatesT[:, s * 512:(s + 1) * 512],
                    start=True, stop=True),
                    reads=B_vsb + [B_w["selm"]], writes=[bh])
                g.add("vector", lambda e, s=s, pbh=pbh, gaterep=gaterep: e.tensor_copy(
                    gaterep[:, s * 512:(s + 1) * 512], pbh),
                    reads=[bh], writes=[B_qdup[0]])
            downb = ET[0]
            g.add("vector", lambda e, downb=downb, gaterep=gaterep: e.tensor_tensor(
                downb[:], ps_o[:], gaterep[:], op=ALU.mult),
                reads=[B_pso, B_qdup[0]], writes=[B_ET[0]])
            pend_stats = []
            for tc in range(4):
                for ot in range(8):
                    slot = (tc * 8 + ot) % 4
                    pb = (ps_a, ps_a, ps_b, ps_b)[slot][:, (slot % 2) * 512:
                                                        (slot % 2) * 512 + 512]
                    bps = (B_psa2[0], B_psa2[1], B_psb2[0], B_psb2[1])[slot]
                    for d in range(DT):
                        g.add("tensor", lambda e, pb=pb, ot=ot, tc=tc, d=d: e.matmul(
                            pb, wwt[:, d * 1024 + ot * 128:d * 1024 + (ot + 1) * 128],
                            hs(d, tc * 512, (tc + 1) * 512),
                            start=(d == 0), stop=False),
                            reads=[B_hT[d], B_w["wwt"]], writes=[bps])
                    g.add("tensor", lambda e, pb=pb, ot=ot, tc=tc, downb=downb: e.matmul(
                        pb, wup[:, ot * 128:(ot + 1) * 128],
                        downb[:, tc * 512:(tc + 1) * 512],
                        start=False, stop=True),
                        reads=[B_ET[0], B_w["wup"]], writes=[bps])
                    g.add("vector", lambda e, pb=pb, ot=ot, tc=tc: e.tensor_tensor(
                        xs(ot, tc * 512, (tc + 1) * 512),
                        xs(ot, tc * 512, (tc + 1) * 512), pb, op=ALU.add),
                        reads=[bps, B_xT[ot][tc]], writes=[B_xT[ot][tc]])
                yv = dr[f"y{ib}"].ap()
                g.add("dma_s", lambda e, yv=yv, tc=tc: e.dma_start(
                    yv[:, tc * 4096:(tc + 1) * 4096],
                    xT[:, tc * 4096:(tc + 1) * 4096]),
                    reads=[B_xT[d][tc] for d in range(DT)])
                if ib + 1 < NB:
                    xv = dr[f"x{ib + 1}"].ap()
                    g.add("dma_s", lambda e, xv=xv, tc=tc: e.dma_start(
                        xT[:, tc * 4096:(tc + 1) * 4096],
                        xv[:, tc * 4096:(tc + 1) * 4096]),
                        writes=[B_xT[d][tc] for d in range(DT)])
                    pend_stats.append(tc)
                while pend_stats and tc - pend_stats[0] >= 2:
                    ln_stats(pend_stats.pop(0))
            for tcp in pend_stats:
                ln_stats(tcp)

        for ib in range(NB):
            if ib == 0:
                xv = dr["x0"].ap()
                for tc in range(4):
                    g.add("dma_s", lambda e, xv=xv, tc=tc: e.dma_start(
                        xT[:, tc * 4096:(tc + 1) * 4096],
                        xv[:, tc * 4096:(tc + 1) * 4096]),
                        writes=[B_xT[d][tc] for d in range(DT)])
                for tc in range(4):
                    ln_stats(tc)
            ln_apply()
            for h in range(H):
                attn_head(ib, h)
                if ib == 0 and h == 0:
                    prep_emit()
                if h % 2 == 1:
                    ln_stats(h // 2)
            ln_apply()
            ffn_phase(ib)

        if DEBUG_TAPS:
            for tap in DEBUG_TAPS.split(","):
                if tap == "hT":
                    dbg = nc.dram_tensor("dbg_hT", [128, DT * C], F32,
                                         kind="ExternalOutput")
                    g.add("dma_s", lambda e, dbg=dbg: e.dma_start(
                        dbg.ap(), hT[:]), reads=B_hT)

        g.finalize()

        sems = {}
        for chan in ENGS:
            sems[chan] = ctx.enter_context(nc.semaphore(f"sem_{chan}"))
        dma_sems = [t.sem for t in g.tasks if t.chan in DMAQ]
        for sname in dma_sems:
            sems[sname] = ctx.enter_context(nc.semaphore(f"sem_{sname}"))

        def epilogue(e):
            for sname in dma_sems:
                e.wait_ge(sems[sname], 16)
            return None
        t = Task(len(g.tasks), "sync", epilogue)
        t.sem = "sync"
        g.tasks.append(t)

        g.emit(nc, sems)
    return nc


_NC_CACHE = None


def kernel(**inputs):
    global _NC_CACHE
    in_maps = host_prep(inputs)
    if _NC_CACHE is None:
        _NC_CACHE = build_program()
    nc = _NC_CACHE
    res = run_bass_kernel_spmd(nc, in_maps, core_ids=list(range(NCORES)))
    out = np.empty((NCORES * NB, C, D), dtype=np.float32)
    for c in range(NCORES):
        for ib in range(NB):
            yb = res.results[c][f"y{ib}"]
            out[NB * c + ib] = yb.reshape(128, 4, 8, 512).transpose(
                1, 3, 2, 0).reshape(C, D)
    return out
